# version 18
# baseline (speedup 1.0000x reference)
import contextlib

import numpy as np

import concourse.bass as bass
import concourse.mybir as mybir
from concourse.bass_utils import run_bass_kernel_spmd

F32 = mybir.dt.float32
BF16 = mybir.dt.bfloat16
AF = mybir.ActivationFunctionType
ALU = mybir.AluOpType
AX = mybir.AxisListType

H = 4096
DIN = 8192
NST = 16
RNK = 256
TOK = 2048
NHC = H // 128
NCC = DIN // 128
NE = 8
ER = 64
EPS_LN = 1e-5
EPS_RMS = 1e-6


class B:
    def __init__(self):
        self.nc = bass.Bass("TRN2", target_bir_lowering=False, debug=False)
        self.E = {"sync": [], "gp": [], "vec": [], "sca": [], "ten": []}
        self.cnt = {}
        self.es = contextlib.ExitStack()

    def em(self, eng, fn):
        self.E[eng].append(fn)

    def scope(self, name):
        for eng in self.E:
            self.E[eng].append(("SCOPE", name))

    def inc(self, sem, n):
        k = sem.name
        self.cnt[k] = self.cnt.get(k, 0) + n
        return self.cnt[k]

    def val(self, sem):
        return self.cnt.get(sem.name, 0)

    def wait(self, eng, sem, v):
        if sem.name == "s_ld":
            return self.ldbar(eng)
        if sem.name == "s_gld":
            return self.gldbar(eng)
        if sem.name == "s_st":
            return self.stbar(eng)
        if v > 0:
            self.em(eng, lambda e, s=sem, vv=v: e.wait_ge(s, vv))

    def _ring_dma(self, q, ring, dst, src):
        i = self.rix[ring] % 4
        self.rix[ring] += 1
        sem = self.rings[ring][i]
        prev = self.val(sem)
        if prev > 0:
            self.em(q, lambda e, s=sem, vv=prev: e.wait_ge(s, vv))
        v = self.inc(sem, 16)
        self.em(q, lambda e, d=dst, s=src, sm=sem: e.dma_start(out=d, in_=s).then_inc(sm, 16))
        return (sem, v)

    def dma(self, q, dst, src):
        return self._ring_dma(q, "ld" if q == "sync" else "gld", dst, src)

    def store(self, dst, src):
        return self._ring_dma("sync", "st", dst, src)

    def waitt(self, eng, tok):
        if tok is None:
            return
        sem, v = tok
        self.em(eng, lambda e, s=sem, vv=v: e.wait_ge(s, vv))

    def ringbar(self, eng, ring):
        for sem in self.rings[ring]:
            v = self.val(sem)
            if v > 0:
                self.em(eng, lambda e, s=sem, vv=v: e.wait_ge(s, vv))

    def ldbar(self, eng):
        self.ringbar(eng, "ld")

    def gldbar(self, eng):
        self.ringbar(eng, "gld")

    def op(self, eng, fn, inc=True):
        sem = self.esem[eng]
        prev = self.val(sem)
        if eng in ("vec", "sca", "gp") and prev > 0:
            self.em(eng, lambda e, s=sem, vv=prev: e.wait_ge(s, vv))
        v = self.inc(sem, 1)
        self.em(eng, lambda e, f=fn, s=sem: f(e).then_inc(s, 1))
        return v

    def vbar(self, eng):
        self.wait(eng, self.s_vec, self.val(self.s_vec))

    def sbar(self, eng):
        self.wait(eng, self.s_sca, self.val(self.s_sca))

    def tbar(self, eng):
        self.wait(eng, self.s_mm, self.val(self.s_mm))

    def stbar(self, eng):
        self.ringbar(eng, "st")

    def fence(self):
        self.ldbar("sync")
        self.stbar("sync")
        self.gldbar("gp")
        tot = self.inc(self.s_f, 5)
        for eng in ("sync", "gp", "vec", "sca", "ten"):
            self.em(eng, lambda e: e.sem_inc(self.s_f, 1))
        for eng in ("sync", "gp", "vec", "sca", "ten"):
            self.wait(eng, self.s_f, tot)

    def sb(self, name, shape, dtype):
        return self.es.enter_context(self.nc.sbuf_tensor(name, list(shape), dtype))

    def psum(self, name, shape):
        return self.es.enter_context(self.nc.psum_tensor(name, list(shape), F32))

    def sem(self, name):
        return self.es.enter_context(self.nc.semaphore(name))

    def din(self, name, shape):
        return self.nc.dram_tensor(name, list(shape), F32, kind="ExternalInput").ap()

    def scr(self, name, shape, dtype):
        return self.nc.dram_tensor(name, list(shape), dtype, kind="Internal").ap()


def build(phases=8, probe=None, no_ffn=False):
    b = B()
    nc = b.nc

    xT = b.din("xT", [H, TOK])
    xTp = b.din("xTp", [H, TOK])
    parity = b.din("parity", [128, 1])
    Win = b.din("Win", [128, 2 * NCC, NHC * 128])
    WoutT = b.din("WoutT", [128, NHC, NCC * 128])
    xpwT = b.din("xpwT", [DIN, 288])
    dtwT = b.din("dtwT", [128, 2 * NCC * 128])
    lAT = b.din("lAT", [128, NE * NHC * ER])
    lBT = b.din("lBT", [NE, ER, H])
    gateT = b.din("gateT", [H, NE])
    convw = b.din("convw", [128, NCC * 4])
    convb = b.din("convb", [128, NCC])
    dtbias = b.din("dtbias", [128, NCC])
    Dpar = b.din("Dpar", [128, NCC])
    AlogT = b.din("AlogT", [128, NCC * NST])
    dtlnw = b.din("dtlnw", [128, 2])
    Blnw = b.din("Blnw", [16, 1])
    Clnw = b.din("Clnw", [16, 1])
    ident8 = b.din("ident8", [8, 8])
    idw128 = b.din("idw128", [128, 128])
    ones1 = b.din("ones1", [128, 1])
    out = nc.dram_tensor("out", [H, TOK], F32, kind="ExternalOutput").ap()

    Winb_d = b.scr("Winb_d", [128, 2 * NCC, NHC * 128], BF16)
    Woutb_d = b.scr("Woutb_d", [128, NHC, NCC * 128], BF16)
    x1_d = b.scr("x1_d", [H, TOK], F32)
    ln1_d = b.scr("ln1_d", [H, TOK], BF16)
    ln1p_d = b.scr("ln1p_d", [H, TOK], BF16)
    xh_d = b.scr("xh_d", [DIN, 1024], BF16)
    xc_d = b.scr("xc_d", [DIN, TOK], BF16)
    xcp_d = b.scr("xcp_d", [DIN, TOK], BF16)
    z_d = b.scr("z_d", [DIN, TOK], BF16)
    dt_d = b.scr("dt_d", [DIN, TOK], BF16)
    dtx_d = b.scr("dtx_d", [DIN, TOK], BF16)
    dtp_d = b.scr("dtp_d", [DIN, TOK], BF16)
    dtxp_d = b.scr("dtxp_d", [DIN, TOK], BF16)
    Bn_d = b.scr("Bn_d", [16, TOK], F32)
    Cn_d = b.scr("Cn_d", [16, TOK], F32)
    Bnp_d = b.scr("Bnp_d", [16, TOK], F32)
    cum_d = b.scr("cum_d", [DIN, TOK], F32)
    yg_d = b.scr("yg_d", [DIN, TOK], BF16)
    rows_d = b.scr("rows_d", [8, TOK], F32)
    w8_d = b.scr("w8_d", [8, TOK], BF16)
    x2_d = b.scr("x2_d", [H, TOK], F32)

    b.s_ld = b.sem("s_ld")
    b.s_gld = b.sem("s_gld")
    b.s_st = b.sem("s_st")
    b.s_mm = b.sem("s_mm")
    b.s_vec = b.sem("s_vec")
    b.s_sca = b.sem("s_sca")
    b.s_gp = b.sem("s_gp")
    b.s_f = b.sem("s_f")
    b.esem = {"vec": b.s_vec, "sca": b.s_sca, "gp": b.s_gp, "ten": b.s_mm}
    b.rings = {
        "ld": [b.sem(f"r_ld{i}") for i in range(4)],
        "gld": [b.sem(f"r_gld{i}") for i in range(4)],
        "st": [b.sem(f"r_st{i}") for i in range(4)],
    }
    b.rix = {"ld": 0, "gld": 0, "st": 0}

    convw_s = b.sb("convw_s", [128, NCC * 4], F32)
    convb_s = b.sb("convb_s", [128, NCC], F32)
    dtbias_s = b.sb("dtbias_s", [128, NCC], F32)
    Dpar_s = b.sb("Dpar_s", [128, NCC], F32)
    A_s = b.sb("A_s", [128, NCC * NST], F32)
    cumend_s = b.sb("cumend_s", [128, NCC], F32)
    dtlnw_s = b.sb("dtlnw_s", [128, 2], F32)
    Blnw_s = b.sb("Blnw_s", [16, 1], F32)
    Clnw_s = b.sb("Clnw_s", [16, 1], F32)
    par_s = b.sb("par_s", [128, 1], F32)
    epsln_s = b.sb("epsln_s", [128, 1], F32)
    id8_s = b.sb("id8_s", [8, 8], BF16)
    idw_s = b.sb("idw_s", [128, 128], BF16)
    ones_s = b.sb("ones_s", [128, 1], BF16)
    onesf_s = b.sb("onesf_s", [128, 1], F32)
    gate_s = b.sb("gate_s", [128, NHC * NE], BF16)
    halo_s = b.sb("halo_s", [128, NCC * 3], F32)
    state_s = b.sb("state_s", [128, NCC * NST], F32)
    r_m = b.sb("r_m", [1, 512], F32)
    r_mq = b.sb("r_mq", [1, 512], F32)
    ab_a = b.sb("ab_a", [128, 512], F32)
    ab_b = b.sb("ab_b", [128, 512], F32)
    r_a = ab_a[0:1, :]
    r_b = ab_b[0:1, :]
    g_sb = b.sb("g_sb", [8, 512], BF16)
    p8_t = b.sb("p8_t", [128, 32], F32)
    tcol = [b.sb(f"tcol{i}", [128, 1], F32) for i in range(12)]
    scol = [b.sb(f"scol{i}", [128, 1], F32) for i in range(8)]
    tmpc = b.sb("tmpc", [128, 1], F32)
    m1_t = b.sb("m1_t", [128, 4], F32)
    m2_t = b.sb("m2_t", [128, 4], F32)
    nm1_t = b.sb("nm1_t", [128, 4], F32)
    e8_t = b.sb("e8_t", [128, 32], F32)
    msk_t = b.sb("msk_t", [128, 32], F32)
    ssum_t = b.sb("ssum_t", [128, 4], F32)
    srec_t = b.sb("srec_t", [128, 4], F32)
    w8_t = b.sb("w8_t", [128, 32], BF16)
    wbc_t = b.sb("wbc_t", [64, NE * 512], BF16)
    us_t = b.sb("us_t", [64, NE * 512], BF16)
    lb_st = [b.sb("lb_st0", [64, 8 * 512], BF16)]

    BUF1 = b.sb("BUF1", [128, 16384], F32)
    BUF2 = b.sb("BUF2", [128, 16384], F32)
    BUF3 = b.sb("BUF3", [128, 8192], F32)

    pA0 = b.psum("pA0", [128, 1024])
    pA1 = b.psum("pA1", [128, 1024])
    pB = b.psum("pB", [128, 1024])
    pC = b.psum("pC", [128, 512])
    pD = b.psum("pD", [128, 512])

    b.dma("sync", convw_s[:], convw)
    b.dma("sync", convb_s[:], convb)
    b.dma("sync", dtbias_s[:], dtbias)
    b.dma("sync", Dpar_s[:], Dpar)
    b.dma("sync", A_s[:], AlogT)
    b.dma("sync", dtlnw_s[:], dtlnw)
    b.dma("sync", Blnw_s[:], Blnw)
    b.dma("sync", Clnw_s[:], Clnw)
    b.dma("sync", par_s[:], parity)
    v0 = b.dma("sync", onesf_s[:], ones1)
    b.dma("gp", id8_s[:], ident8)
    b.dma("gp", idw_s[:], idw128)
    b.dma("gp", ones_s[:], ones1)
    b.dma("gp", gate_s[:].rearrange("p (a e) -> p a e", a=NHC),
          gateT.rearrange("(a p) e -> p a e", p=128))
    b.op("vec", lambda e: e.memset(epsln_s[:], EPS_LN))
    b.wait("sca", b.s_ld, v0)
    b.op("sca", lambda e: e.activation(A_s[:], A_s[:], AF.Exp, scale=1.0))
    vA = b.op("sca", lambda e: e.mul(A_s[:], A_s[:], -1.0), inc=True)
    b.fence()

    def emit_ln(xin_t, lnout_t, xsq, lt):
        v_sq = b.op("sca", lambda e: e.activation(xsq, xin_t, AF.Square), inc=True)
        b.vbar("ten")
        b.wait("ten", b.s_sca, v_sq)

        def mms(e):
            for hc in range(NHC):
                e.matmul(pC[0:1, :], onesf_s[:], xin_t[:, hc * 512:(hc + 1) * 512],
                         start=(hc == 0), stop=(hc == NHC - 1))
            for hc in range(NHC - 1):
                e.matmul(pD[0:1, :], ones_s[:], xsq[:, hc * 512:(hc + 1) * 512],
                         start=(hc == 0), stop=False)
            return e.matmul(pD[0:1, :], ones_s[:],
                            xsq[:, (NHC - 1) * 512:NHC * 512], start=False, stop=True)
        v_mm = b.op("ten", mms, inc=True)
        b.wait("vec", b.s_mm, v_mm)
        b.op("vec", lambda e: e.tensor_scalar_mul(r_m[:], pC[0:1, :], 1.0 / H))
        b.op("vec", lambda e: e.tensor_scalar_mul(r_mq[:], pD[0:1, :], 1.0 / H))
        b.op("vec", lambda e: e.tensor_tensor(r_a, r_m[:], r_m[:], ALU.mult))
        v_v = b.op("vec", lambda e: e.tensor_tensor(r_mq[:], r_mq[:], r_a,
                                                    ALU.subtract), inc=True)
        b.wait("sca", b.s_vec, v_v)
        v_sd = b.op("sca", lambda e: e.activation(r_mq[:], r_mq[:], AF.Sqrt,
                                                  bias=epsln_s[0:1, :]), inc=True)
        b.wait("vec", b.s_sca, v_sd)
        b.stbar("vec")
        b.op("vec", lambda e: e.reciprocal(r_a, r_mq[:]))
        v_ab = b.op("vec", lambda e: e.scalar_tensor_tensor(
            r_b, r_m[:], -1.0, r_a, ALU.mult, ALU.mult), inc=True)
        b.wait("sync", b.s_vec, v_ab)
        b.store(rows_d[0:1, lt * 512:(lt + 1) * 512], r_a)
        v_s = b.store(rows_d[1:2, lt * 512:(lt + 1) * 512], r_b)
        b.wait("sync", b.s_st, v_s)
        v_a = b.dma("sync", ab_a[:], rows_d[0:1, lt * 512:(lt + 1) * 512].partition_broadcast(128))
        v_b = b.dma("sync", ab_b[:], rows_d[1:2, lt * 512:(lt + 1) * 512].partition_broadcast(128))
        b.waitt("vec", v_a)
        b.waitt("vec", v_b)
        a_bc = ab_a[:].unsqueeze(1).broadcast_to([128, NHC, 512])
        b_bc = ab_b[:].unsqueeze(1).broadcast_to([128, NHC, 512])
        x3 = xin_t.rearrange("p (a l) -> p a l", a=NHC)
        q3 = xsq.rearrange("p (a l) -> p a l", a=NHC)
        l3 = lnout_t.rearrange("p (a l) -> p a l", a=NHC)
        b.op("vec", lambda e: e.tensor_tensor(q3, x3, a_bc, ALU.mult))
        return b.op("vec", lambda e: e.tensor_tensor(l3, q3, b_bc, ALU.add), inc=True)

    def emit_ffn(x_t, ln_t, lt, lA_v):
        b.vbar("ten")

        def mm_g(e):
            for hc in range(NHC - 1):
                e.matmul(pC[0:8, :], gate_s[:, hc * 8:(hc + 1) * 8],
                         ln_t[:, hc * 512:(hc + 1) * 512], start=(hc == 0), stop=False)
            return e.matmul(pC[0:8, :], gate_s[:, (NHC - 1) * 8:NHC * 8],
                            ln_t[:, (NHC - 1) * 512:NHC * 512], start=False, stop=True)
        v_g = b.op("ten", mm_g, inc=True)
        b.wait("sca", b.s_mm, v_g)
        v_ge = b.op("sca", lambda e: e.copy(g_sb[:], pC[0:8, :]), inc=True)
        b.wait("ten", b.s_sca, v_ge)
        for s4 in range(4):
            b.vbar("ten")
            v_t8 = b.op("ten", lambda e, s=s4: e.matmul(
                pD[:, 0:8], g_sb[:, s * 128:(s + 1) * 128], id8_s[:],
                start=True, stop=True), inc=True)
            b.wait("vec", b.s_mm, v_t8)
            b.op("vec", lambda e, s=s4: e.tensor_copy(p8_t[:, s * 8:(s + 1) * 8], pD[:, 0:8]),
                 inc=True)
        for s4 in range(4):
            p = p8_t[:, s4 * 8:(s4 + 1) * 8]
            V = lambda f: b.op("vec", f)

            def c(t, i=0):
                return t[:, i:i + 1]
            V(lambda e, p=p: e.tensor_tensor(c(tcol[0]), p[:, 0:1], p[:, 1:2], ALU.max))
            V(lambda e, p=p: e.tensor_tensor(c(tcol[1]), p[:, 2:3], p[:, 3:4], ALU.max))
            V(lambda e, p=p: e.tensor_tensor(c(tcol[2]), p[:, 4:5], p[:, 5:6], ALU.max))
            V(lambda e, p=p: e.tensor_tensor(c(tcol[3]), p[:, 6:7], p[:, 7:8], ALU.max))
            V(lambda e, p=p: e.tensor_tensor(c(tcol[4]), p[:, 0:1], p[:, 1:2], ALU.min))
            V(lambda e, p=p: e.tensor_tensor(c(tcol[5]), p[:, 2:3], p[:, 3:4], ALU.min))
            V(lambda e, p=p: e.tensor_tensor(c(tcol[6]), p[:, 4:5], p[:, 5:6], ALU.min))
            V(lambda e, p=p: e.tensor_tensor(c(tcol[7]), p[:, 6:7], p[:, 7:8], ALU.min))
            V(lambda e: e.tensor_tensor(c(tcol[8]), c(tcol[0]), c(tcol[1]), ALU.max))
            V(lambda e: e.tensor_tensor(c(tcol[9]), c(tcol[2]), c(tcol[3]), ALU.max))
            V(lambda e, s=s4: e.tensor_tensor(m1_t[:, s:s + 1], c(tcol[8]), c(tcol[9]), ALU.max))
            V(lambda e: e.tensor_tensor(c(tcol[10]), c(tcol[8]), c(tcol[9]), ALU.min))
            V(lambda e: e.tensor_tensor(c(tcol[11]), c(tcol[8]), c(tcol[9]), ALU.is_ge))
            V(lambda e: e.tensor_tensor(c(scol[0]), c(tcol[0]), c(tcol[1]), ALU.min))
            V(lambda e: e.tensor_tensor(c(scol[1]), c(tcol[2]), c(tcol[3]), ALU.min))
            V(lambda e: e.tensor_tensor(c(tmpc), c(scol[0]), c(scol[1]), ALU.subtract))
            V(lambda e: e.tensor_tensor(c(tmpc), c(tcol[11]), c(tmpc), ALU.mult))
            V(lambda e: e.tensor_tensor(c(scol[2]), c(scol[1]), c(tmpc), ALU.add))
            V(lambda e: e.tensor_tensor(c(scol[3]), c(tcol[0]), c(tcol[1]), ALU.is_ge))
            V(lambda e: e.tensor_tensor(c(tmpc), c(tcol[4]), c(tcol[5]), ALU.subtract))
            V(lambda e: e.tensor_tensor(c(tmpc), c(scol[3]), c(tmpc), ALU.mult))
            V(lambda e: e.tensor_tensor(c(scol[4]), c(tcol[5]), c(tmpc), ALU.add))
            V(lambda e: e.tensor_tensor(c(scol[5]), c(tcol[2]), c(tcol[3]), ALU.is_ge))
            V(lambda e: e.tensor_tensor(c(tmpc), c(tcol[6]), c(tcol[7]), ALU.subtract))
            V(lambda e: e.tensor_tensor(c(tmpc), c(scol[5]), c(tmpc), ALU.mult))
            V(lambda e: e.tensor_tensor(c(scol[6]), c(tcol[7]), c(tmpc), ALU.add))
            V(lambda e: e.tensor_tensor(c(tmpc), c(scol[4]), c(scol[6]), ALU.subtract))
            V(lambda e: e.tensor_tensor(c(tmpc), c(tcol[11]), c(tmpc), ALU.mult))
            V(lambda e: e.tensor_tensor(c(scol[7]), c(scol[6]), c(tmpc), ALU.add))
            V(lambda e: e.tensor_tensor(c(tcol[10]), c(tcol[10]), c(scol[2]), ALU.max))
            V(lambda e, s=s4: e.tensor_tensor(m2_t[:, s:s + 1], c(tcol[10]), c(scol[7]), ALU.max))
        v_nm = b.op("vec", lambda e: e.tensor_scalar_mul(nm1_t[:], m1_t[:], -1.0), inc=True)
        b.wait("sca", b.s_vec, v_nm)
        v_e8 = None
        for s4 in range(4):
            v_e8 = b.op("sca", lambda e, s=s4: e.activation(
                e8_t[:, s * 8:(s + 1) * 8], p8_t[:, s * 8:(s + 1) * 8], AF.Exp,
                bias=nm1_t[:, s:s + 1]), inc=True)
        b.wait("vec", b.s_sca, v_e8)
        for s4 in range(4):
            b.op("vec", lambda e, s=s4: e.tensor_scalar(
                msk_t[:, s * 8:(s + 1) * 8], p8_t[:, s * 8:(s + 1) * 8],
                m2_t[:, s:s + 1], None, ALU.is_ge))
            b.op("vec", lambda e, s=s4: e.tensor_tensor(
                msk_t[:, s * 8:(s + 1) * 8], msk_t[:, s * 8:(s + 1) * 8],
                e8_t[:, s * 8:(s + 1) * 8], ALU.mult))
            b.op("vec", lambda e, s=s4: e.reduce_sum(
                ssum_t[:, s:s + 1], msk_t[:, s * 8:(s + 1) * 8], axis=AX.X))
            b.op("vec", lambda e, s=s4: e.reciprocal(srec_t[:, s:s + 1], ssum_t[:, s:s + 1]))
            b.op("vec", lambda e, s=s4: e.tensor_scalar_mul(
                w8_t[:, s * 8:(s + 1) * 8], msk_t[:, s * 8:(s + 1) * 8], srec_t[:, s:s + 1]))
        v_w8 = b.op("vec", lambda e: e.tensor_copy(w8_t[:, 0:1], w8_t[:, 0:1]), inc=True)
        b.wait("ten", b.s_vec, v_w8)
        b.sbar("ten")
        v_wT = None
        for s4 in range(4):
            v_wT = b.op("ten", lambda e, s=s4: e.matmul(
                pC[0:8, s * 128:(s + 1) * 128], w8_t[:, s * 8:(s + 1) * 8], idw_s[:],
                start=True, stop=True), inc=True)
        b.wait("sca", b.s_mm, v_wT)
        v_we = b.op("sca", lambda e: e.copy(g_sb[:], pC[0:8, :]), inc=True)
        b.wait("sync", b.s_sca, v_we)
        v_ws = b.store(w8_d[:, lt * 512:(lt + 1) * 512], g_sb[:])
        b.wait("sync", b.s_st, v_ws)
        v_wb = b.dma("sync", wbc_t[:].rearrange("p (e l) -> p e l", e=NE),
                     w8_d[:, lt * 512:(lt + 1) * 512].unsqueeze(0)
                     .broadcast_to([64, 8, 512]))
        b.wait("ten", b.s_sca, v_we)
        v_us = None
        for ei in range(NE):
            def mm_d(e, ei=ei):
                for hc in range(NHC - 1):
                    e.matmul(pC[0:64, :], lA_v[:, (ei * NHC + hc) * ER:(ei * NHC + hc + 1) * ER],
                             ln_t[:, hc * 512:(hc + 1) * 512], start=(hc == 0), stop=False)
                return e.matmul(pC[0:64, :],
                                lA_v[:, (ei * NHC + NHC - 1) * ER:(ei * NHC + NHC) * ER],
                                ln_t[:, (NHC - 1) * 512:NHC * 512], start=False, stop=True)
            v_d = b.op("ten", mm_d, inc=True)
            if ei == 0:
                b.waitt("vec", v_wb)
            b.wait("vec", b.s_mm, v_d)
            v_us = b.op("vec", lambda e, ei=ei: e.tensor_tensor(
                us_t[:, ei * 512:(ei + 1) * 512], pC[0:64, :],
                wbc_t[:, ei * 512:(ei + 1) * 512], ALU.mult), inc=True)
            b.wait("ten", b.s_vec, v_us)
        v_last = None
        for hg in range(NHC // 4):
            lbuf = lb_st[0]
            b.tbar("gp")
            v_lb = b.dma("gp", lbuf[:].rearrange("p (e c) -> p e c", e=NE),
                         lBT[:, :, hg * 512:(hg + 1) * 512].rearrange("e p c -> p e c"))
            b.waitt("ten", v_lb)

            def mm_u(e, hg=hg, lbuf=lbuf):
                last = None
                pslots = [pA0[:, 0:512], pA0[:, 512:1024], pA1[:, 0:512], pA1[:, 512:1024]]
                for ei in range(NE):
                    for hh in range(4):
                        last = e.matmul(pslots[hh],
                                        lbuf[:, ei * 512 + hh * 128: ei * 512 + (hh + 1) * 128],
                                        us_t[:, ei * 512:(ei + 1) * 512],
                                        start=(ei == 0), stop=(ei == NE - 1))
                return last
            v_u = b.op("ten", mm_u, inc=True)
            b.wait("vec", b.s_mm, v_u)
            pslots = [pA0[:, 0:512], pA0[:, 512:1024], pA1[:, 0:512], pA1[:, 512:1024]]
            for hh in range(4):
                hc = hg * 4 + hh
                v_last = b.op("vec", lambda e, hh=hh, hc=hc: e.scalar_tensor_tensor(
                    x_t[:, hc * 512:(hc + 1) * 512], pslots[hh], 0.5,
                    x_t[:, hc * 512:(hc + 1) * 512], ALU.mult, ALU.add), inc=True)
            b.wait("ten", b.s_vec, v_last)
        return v_last

    def phaseA(x_src, res_dst, postln_dst, lA_v):
        x_t = BUF1[:]
        ln_t = BUF2[:, 0:8192].bitcast(BF16)
        xsq = BUF2[:, 8192:16384].bitcast(BF16)
        for lt in range(4):
            b.stbar("sync")
            v_x = b.dma("sync", x_t.rearrange("p (a l) -> p a l", a=NHC),
                        x_src[:, lt * 512:(lt + 1) * 512]
                        .rearrange("(a p) l -> p a l", p=128))
            b.waitt("vec", v_x)
            b.waitt("sca", v_x)
            b.waitt("ten", v_x)
            v_ln = emit_ln(x_t, ln_t, xsq, lt)
            b.wait("sca", b.s_vec, v_ln)
            b.wait("ten", b.s_vec, v_ln)
            if no_ffn:
                v_f = v_ln
            else:
                v_f = emit_ffn(x_t, ln_t, lt, lA_v)
            if res_dst is not None:
                b.wait("sync", b.s_vec, v_f)
                b.store(res_dst[:, lt * 512:(lt + 1) * 512]
                        .rearrange("(a p) l -> p a l", p=128),
                        x_t.rearrange("p (a l) -> p a l", a=NHC))
            if postln_dst is not None:
                b.wait("sca", b.s_vec, v_f)
                v_ln2 = emit_ln(x_t, ln_t, xsq, lt)
                b.wait("sync", b.s_vec, v_ln2)
                b.store(postln_dst[:, lt * 512:(lt + 1) * 512]
                        .rearrange("(a p) l -> p a l", p=128),
                        ln_t.rearrange("p (a l) -> p a l", a=NHC))
        b.fence()

    lA_v = BUF3[:].bitcast(BF16)
    vla = b.dma("gp", lA_v, lAT)
    b.waitt("ten", vla)

    if phases >= 1:
        b.scope("A")
        phaseA(xT, x1_d, ln1_d, lA_v)
    if phases >= 2:
        b.scope("Adup")
        phaseA(xTp, None, ln1p_d, lA_v)

    def phB(ln_dram, is_dup):
        lnslab = BUF1[:].bitcast(BF16)
        wb = [BUF3[:, 0:2048].bitcast(BF16), BUF3[:, 2048:4096].bitcast(BF16)]
        xpw_s = BUF3[:, 4096:4672].bitcast(BF16)
        xh_t = BUF2[:, 0:1024].bitcast(BF16)
        xc_t = [BUF2[:, 1024 + i * 1024: 1024 + (i + 1) * 1024].bitcast(BF16)
                for i in range(4)]
        xp_acc = BUF2[:, 5120:9216]
        dte_t = BUF2[:, 9216:11264]
        zt_t = BUF2[:, 9216:10240].bitcast(BF16)
        dt_t = BUF2[:, 11264:12288].bitcast(BF16)
        dtx_t = BUF2[:, 12288:13312].bitcast(BF16)
        xcg = BUF2[:, 13312:14336].bitcast(BF16)
        rst_b = BUF2[:, 2048:4096]
        dtr_t = BUF2[:, 4096:6144].bitcast(BF16)
        xsqB = BUF2[:, 0:2048].bitcast(BF16)
        bn_acc = BUF2[0:16, 12288:14336]
        cn_acc = BUF2[0:16, 14336:16384]

        dt_dst = dtp_d if is_dup else dt_d
        dtx_dst = dtxp_d if is_dup else dtx_d
        xc_dst = xcp_d if is_dup else xc_d
        bn_dst = Bnp_d if is_dup else Bn_d

        b.op("vec", lambda e: e.memset(xp_acc, 0.0))
        b.op("vec", lambda e: e.memset(bn_acc, 0.0))
        v_z = b.op("vec", lambda e: e.memset(cn_acc, 0.0), inc=True)
        b.wait("ten", b.s_vec, v_z)

        n_cc = NCC if is_dup else 2 * NCC
        prevB = {}
        for half in range(2):
            b.stbar("sync")
            v_slab = b.dma("sync", lnslab[:, 0:NHC * 1024]
                           .rearrange("p (a l) -> p a l", a=NHC),
                           ln_dram[:, half * 1024:(half + 1) * 1024]
                           .rearrange("(a p) l -> p a l", p=128))
            b.waitt("ten", v_slab)
            for cc in range(n_cc):
                is_x = cc < NCC
                buf = wb[cc % 2]
                if ("mm", cc % 2) in prevB:
                    b.wait("gp", b.s_mm, prevB[("mm", cc % 2)])
                v_w = b.dma("gp", buf, Win[:, cc, :])
                b.waitt("ten", v_w)
                if ("ev", cc % 2) in prevB:
                    b.wait("ten", b.s_sca, prevB[("ev", cc % 2)])
                pp = pA0 if (cc % 2 == 0) else pA1

                def mm_ip(e, buf=buf, pp=pp):
                    last = None
                    for lq in range(2):
                        for hc in range(NHC):
                            last = e.matmul(pp[:, lq * 512:(lq + 1) * 512],
                                            buf[:, hc * 128:(hc + 1) * 128],
                                            lnslab[:, hc * 1024 + lq * 512:
                                                   hc * 1024 + (lq + 1) * 512],
                                            start=(hc == 0), stop=(hc == NHC - 1))
                    return last
                v_mm = b.op("ten", mm_ip, inc=True)
                prevB[("mm", cc % 2)] = v_mm
                b.wait("sca", b.s_mm, v_mm)
                if is_x and half == 0:
                    b.stbar("sca")
                    v_ev = b.op("sca", lambda e, pp=pp: e.copy(xh_t[:, 0:1024], pp[:]),
                                inc=True)
                    prevB[("ev", cc % 2)] = v_ev
                    b.wait("sync", b.s_sca, v_ev)
                    b.store(xh_d[cc * 128:(cc + 1) * 128, :], xh_t[:, 0:1024])
                elif is_x:
                    b.vbar("sca")
                    v_ev = b.op("sca", lambda e, pp=pp: e.copy(xh_t[:, 1024:2048], pp[:]),
                                inc=True)
                    prevB[("ev", cc % 2)] = v_ev
                    v_x0 = b.dma("sync", xh_t[:, 0:1024], xh_d[cc * 128:(cc + 1) * 128, :])
                    b.wait("vec", b.s_sca, v_ev)
                    b.waitt("vec", v_x0)
                    if is_dup:
                        b.op("vec", lambda e, cc=cc: e.tensor_scalar_mul(
                            halo_s[:, cc * 3:(cc + 1) * 3], xh_t[:, 2045:2048], par_s[:]))
                    b.op("vec", lambda e, cc=cc: e.tensor_scalar(
                        dte_t, xh_t[:, 0:2048], convw_s[:, cc * 4 + 3:cc * 4 + 4],
                        convb_s[:, cc:cc + 1], ALU.mult, ALU.add))
                    for k in range(3):
                        b.op("vec", lambda e, cc=cc, k=k: e.scalar_tensor_tensor(
                            dte_t[:, 3 - k:2048], xh_t[:, 0:2045 + k],
                            convw_s[:, cc * 4 + k:cc * 4 + k + 1],
                            dte_t[:, 3 - k:2048], ALU.mult, ALU.add))
                    if not is_dup:
                        for k in range(3):
                            b.op("vec", lambda e, cc=cc, k=k: e.scalar_tensor_tensor(
                                dte_t[:, 0:3 - k], halo_s[:, cc * 3 + k:(cc + 1) * 3],
                                convw_s[:, cc * 4 + k:cc * 4 + k + 1],
                                dte_t[:, 0:3 - k], ALU.mult, ALU.add))
                    v_cv = b.op("vec", lambda e: e.tensor_copy(tcol[0][:], tcol[0][:]),
                                inc=True)
                    b.wait("sca", b.s_vec, v_cv)
                    b.stbar("sca")
                    v_si = b.op("sca", lambda e, cc=cc: e.activation(
                        xc_t[cc % 4], dte_t, AF.Silu), inc=True)
                    b.wait("sync", b.s_sca, v_si)
                    b.store(xc_dst[cc * 128:(cc + 1) * 128, :], xc_t[cc % 4])
                    if cc % 4 == 3:
                        ccg = cc // 4
                        b.tbar("gp")
                        v_wx = b.dma("gp", xpw_s.rearrange("p (a m) -> p a m", a=4),
                                     xpwT.rearrange("(a p) m -> p a m", p=128)
                                     [:, ccg * 4:(ccg + 1) * 4, :])
                        b.waitt("ten", v_wx)
                        b.wait("ten", b.s_sca, v_si)
                        b.vbar("ten")
                        for lt in range(4):
                            def mm_xp(e, lt=lt):
                                last = None
                                for cci in range(4):
                                    w = xpw_s[:, cci * 288:(cci + 1) * 288]
                                    xcs = xc_t[cci][:, lt * 512:(lt + 1) * 512]
                                    e.matmul(pB[:, 0:512], w[:, 0:128], xcs,
                                             start=(cci == 0), stop=(cci == 3))
                                    e.matmul(pB[:, 512:1024], w[:, 128:256], xcs,
                                             start=(cci == 0), stop=(cci == 3))
                                    e.matmul(pC[0:16, :], w[:, 256:272], xcs,
                                             start=(cci == 0), stop=(cci == 3))
                                    last = e.matmul(pD[0:16, :], w[:, 272:288], xcs,
                                                    start=(cci == 0), stop=(cci == 3))
                                return last
                            v_xp = b.op("ten", mm_xp, inc=True)
                            b.wait("vec", b.s_mm, v_xp)
                            sl = slice(lt * 512, (lt + 1) * 512)
                            b.op("vec", lambda e, sl=sl: e.tensor_tensor(
                                xp_acc[:, sl], xp_acc[:, sl], pB[:, 0:512], ALU.add))
                            b.op("vec", lambda e, sl=sl: e.tensor_tensor(
                                xp_acc[:, 2048 + sl.start:2048 + sl.stop],
                                xp_acc[:, 2048 + sl.start:2048 + sl.stop],
                                pB[:, 512:1024], ALU.add))
                            b.op("vec", lambda e, sl=sl: e.tensor_tensor(
                                bn_acc[:, sl], bn_acc[:, sl], pC[0:16, :], ALU.add))
                            v_fl = b.op("vec", lambda e, sl=sl: e.tensor_tensor(
                                cn_acc[:, sl], cn_acc[:, sl], pD[0:16, :], ALU.add),
                                inc=True)
                            b.wait("ten", b.s_vec, v_fl)
                else:
                    b.stbar("sca")
                    v_ev = b.op("sca", lambda e, pp=pp: e.copy(zt_t[:, 0:1024], pp[:]),
                                inc=True)
                    prevB[("ev", cc % 2)] = v_ev
                    b.wait("sync", b.s_sca, v_ev)
                    b.store(z_d[(cc - NCC) * 128:(cc - NCC + 1) * 128,
                                half * 1024:(half + 1) * 1024],
                            zt_t[:, 0:1024])
        b.fence()

        v_q = b.op("sca", lambda e: e.activation(xsqB, xp_acc, AF.Square), inc=True)
        b.wait("ten", b.s_sca, v_q)
        for lt in range(4):
            b.vbar("ten")

            def mm_ss(e, lt=lt):
                e.matmul(pC[0:1, :], ones_s[:], xsqB[:, lt * 512:(lt + 1) * 512],
                         start=True, stop=False)
                return e.matmul(pC[0:1, :], ones_s[:],
                                xsqB[:, 2048 + lt * 512:2048 + (lt + 1) * 512],
                                start=False, stop=True)
            v_ss = b.op("ten", mm_ss, inc=True)
            b.wait("vec", b.s_mm, v_ss)
            b.op("vec", lambda e: e.tensor_scalar_mul(r_m[:], pC[0:1, :], 1.0 / RNK))
            v_vv = b.op("vec", lambda e: e.tensor_scalar_add(r_m[:], r_m[:], EPS_RMS), inc=True)
            b.wait("sca", b.s_vec, v_vv)
            v_sd = b.op("sca", lambda e: e.activation(r_m[:], r_m[:], AF.Sqrt), inc=True)
            b.wait("vec", b.s_sca, v_sd)
            b.stbar("vec")
            v_rc = b.op("vec", lambda e: e.reciprocal(r_a, r_m[:]), inc=True)
            b.wait("sync", b.s_vec, v_rc)
            v_sr = b.store(rows_d[2:3, lt * 512:(lt + 1) * 512], r_a)
            b.wait("sync", b.s_st, v_sr)
            b.dma("sync", rst_b[:, lt * 512:(lt + 1) * 512],
                  rows_d[2:3, lt * 512:(lt + 1) * 512].partition_broadcast(128))
        b.wait("vec", b.s_ld, b.val(b.s_ld))
        for ch in range(2):
            b.op("vec", lambda e, ch=ch: e.scalar_tensor_tensor(
                dtr_t[:, ch * 2048:(ch + 1) * 2048], xp_acc[:, ch * 2048:(ch + 1) * 2048],
                dtlnw_s[:, ch:ch + 1], rst_b[:, 0:2048], ALU.mult, ALU.mult))
        v_dtr = b.op("vec", lambda e: e.tensor_copy(tcol[0][:], tcol[0][:]), inc=True)

        b.wait("sca", b.s_vec, v_dtr)
        v_qb = b.op("sca", lambda e: e.activation(xsqB[0:16, 0:2048], bn_acc, AF.Square),
                    inc=True)
        v_qc = b.op("sca", lambda e: e.activation(xsqB[0:16, 2048:4096], cn_acc, AF.Square),
                    inc=True)
        b.wait("ten", b.s_sca, v_qc)
        for lt in range(4):
            b.vbar("ten")

            def mm_bc(e, lt=lt):
                e.matmul(pC[0:1, :], ones_s[0:16, :], xsqB[0:16, lt * 512:(lt + 1) * 512],
                         start=True, stop=True)
                return e.matmul(pD[0:1, :], ones_s[0:16, :],
                                xsqB[0:16, 2048 + lt * 512:2048 + (lt + 1) * 512],
                                start=True, stop=True)
            v_bc = b.op("ten", mm_bc, inc=True)
            b.wait("vec", b.s_mm, v_bc)
            b.op("vec", lambda e: e.tensor_scalar_mul(r_m[:], pC[0:1, :], 1.0 / NST))
            b.op("vec", lambda e: e.tensor_scalar_add(r_m[:], r_m[:], EPS_RMS))
            b.op("vec", lambda e: e.tensor_scalar_mul(r_mq[:], pD[0:1, :], 1.0 / NST))
            v_v3 = b.op("vec", lambda e: e.tensor_scalar_add(r_mq[:], r_mq[:], EPS_RMS),
                        inc=True)
            b.wait("sca", b.s_vec, v_v3)
            b.op("sca", lambda e: e.activation(r_m[:], r_m[:], AF.Sqrt))
            v_s2 = b.op("sca", lambda e: e.activation(r_mq[:], r_mq[:], AF.Sqrt), inc=True)
            b.wait("vec", b.s_sca, v_s2)
            b.stbar("vec")
            b.op("vec", lambda e: e.reciprocal(r_a, r_m[:]))
            v_r2 = b.op("vec", lambda e: e.reciprocal(r_b, r_mq[:]), inc=True)
            b.wait("sync", b.s_vec, v_r2)
            b.store(rows_d[3:4, lt * 512:(lt + 1) * 512], r_a)
            v_s4 = b.store(rows_d[4:5, lt * 512:(lt + 1) * 512], r_b)
            b.wait("sync", b.s_st, v_s4)
            b.dma("sync", ab_a[0:16, :],
                  rows_d[3:4, lt * 512:(lt + 1) * 512].partition_broadcast(16))
            v_ab2 = b.dma("sync", ab_b[0:16, :],
                          rows_d[4:5, lt * 512:(lt + 1) * 512].partition_broadcast(16))
            b.wait("vec", b.s_ld, v_ab2)
            b.op("vec", lambda e, lt=lt: e.tensor_tensor(
                bn_acc[:, lt * 512:(lt + 1) * 512], bn_acc[:, lt * 512:(lt + 1) * 512],
                ab_a[0:16, :], ALU.mult))
            v_apl = b.op("vec", lambda e, lt=lt: e.tensor_tensor(
                cn_acc[:, lt * 512:(lt + 1) * 512], cn_acc[:, lt * 512:(lt + 1) * 512],
                ab_b[0:16, :], ALU.mult), inc=True)
            b.wait("sync", b.s_vec, v_apl)
        b.op("vec", lambda e: e.tensor_scalar_mul(bn_acc, bn_acc, Blnw_s[:]))
        v_cn = b.op("vec", lambda e: e.tensor_scalar_mul(cn_acc, cn_acc, Clnw_s[:]),
                    inc=True)
        b.wait("sync", b.s_vec, v_cn)
        b.store(bn_dst[:, :], bn_acc)
        if not is_dup:
            b.store(Cn_d[:, :], cn_acc)

        dtw_v = BUF1[:, 0:8192].bitcast(BF16)
        b.tbar("gp")
        vdw = b.dma("gp", dtw_v, dtwT)
        b.waitt("ten", vdw)
        b.wait("ten", b.s_vec, v_dtr)
        b.stbar("sync")
        for dc in range(NCC):
            b.vbar("sync")
            v_xcl = b.dma("sync", xcg, xc_dst[dc * 128:(dc + 1) * 128, :])
            for hf in range(2):
                b.sbar("ten")

                def mm_dt(e, dc=dc, hf=hf):
                    last = None
                    for lq in range(2):
                        for rc in range(2):
                            last = e.matmul(
                                pB[:, lq * 512:(lq + 1) * 512],
                                dtw_v[:, (rc * NCC + dc) * 128:(rc * NCC + dc + 1) * 128],
                                dtr_t[:, rc * 2048 + hf * 1024 + lq * 512:
                                      rc * 2048 + hf * 1024 + (lq + 1) * 512],
                                start=(rc == 0), stop=(rc == 1))
                    return last
                v_dm = b.op("ten", mm_dt, inc=True)
                b.wait("sca", b.s_mm, v_dm)
                b.op("sca", lambda e, dc=dc, hf=hf: e.activation(
                    dte_t[:, hf * 1024:(hf + 1) * 1024], pB[:], AF.Exp,
                    bias=dtbias_s[:, dc:dc + 1]))
            b.stbar("sca")
            v_sp = b.op("sca", lambda e: e.activation(dt_t, dte_t, AF.Ln, bias=onesf_s[:]), inc=True)
            b.wait("vec", b.s_sca, v_sp)
            b.waitt("vec", v_xcl)
            v_dtx = b.op("vec", lambda e: e.tensor_tensor(dtx_t, dt_t, xcg, ALU.mult),
                         inc=True)
            b.wait("sync", b.s_vec, v_dtx)
            b.store(dt_dst[dc * 128:(dc + 1) * 128, :], dt_t)
            b.store(dtx_dst[dc * 128:(dc + 1) * 128, :], dtx_t)
        b.fence()

    if phases >= 3:
        b.scope("Bdup")
        phB(ln1p_d, True)
    if phases >= 4:
        b.scope("B")
        phB(ln1_d, False)

    def suffix_state():
        onesw = BUF2[:, 0:1024].bitcast(BF16)
        cumt = [BUF2[:, 1024 + i * 2048: 1024 + (i + 1) * 2048] for i in range(2)]
        dtt = [BUF2[:, 5120 + i * 1024: 5120 + (i + 1) * 1024].bitcast(BF16)
               for i in range(2)]
        wsuf = [BUF2[:, 7168 + i * 1024: 7168 + (i + 1) * 1024].bitcast(BF16)
                for i in range(2)]
        dbxp = [BUF2[:, 9216 + i * 1024: 9216 + (i + 1) * 1024].bitcast(BF16)
                for i in range(2)]
        scrx = BUF2[:, 11264:12288].bitcast(BF16)
        dtxt = [BUF2[:, 12288 + i * 1024: 12288 + (i + 1) * 1024].bitcast(BF16)
                for i in range(2)]
        negA_s = BUF3[:, 0:1024]
        biasA_s = BUF3[:, 1024:2048]
        b.op("vec", lambda e: e.memset(onesw, 1.0))
        b.op("vec", lambda e: e.tensor_scalar_mul(negA_s, A_s[:], -1.0))
        pv = {}
        for tp in range(NCC):
            pr = tp % 2
            if ("cum", pr) in pv:
                b.wait("sync", b.s_vec, pv[("cum", pr)])
            v_l = b.dma("sync", dtt[pr], dtp_d[tp * 128:(tp + 1) * 128, :])
            b.waitt("vec", v_l)
            if ("cst", pr) in pv:
                b.waitt("vec", pv[("cst", pr)])
            v_sc = b.op("vec", lambda e, pr=pr: e.tensor_tensor_scan(
                cumt[pr], onesw, dtt[pr], 0.0, ALU.mult, ALU.add), inc=True)
            v_c = b.op("vec", lambda e, tp=tp, pr=pr: e.tensor_copy(
                cumend_s[:, tp:tp + 1], cumt[pr][:, 2047:2048]), inc=True)
            pv[("cum", pr)] = v_c
            b.wait("sync", b.s_vec, v_c)
            pv[("cst", pr)] = b.store(cum_d[tp * 128:(tp + 1) * 128, :], cumt[pr])
        for n in range(NST):
            b.op("vec", lambda e, n=n: e.tensor_tensor(
                biasA_s.rearrange("p (a s) -> p a s", s=NST)[:, :, n],
                A_s[:].rearrange("p (a s) -> p a s", s=NST)[:, :, n],
                cumend_s[:], ALU.mult))
        v_ba = b.op("vec", lambda e: e.tensor_copy(tcol[0][:], tcol[0][:]), inc=True)
        b.wait("sca", b.s_vec, v_ba)
        brep = BUF1[:].bitcast(BF16)
        v_br = b.dma("gp", brep.rearrange("p (s l) -> p s l", s=NST),
                     Bnp_d.unsqueeze(0).broadcast_to([128, 16, TOK]))
        b.waitt("vec", v_br)
        b.waitt("gp", v_br)
        b.stbar("sync")
        pv = {}
        for tp in range(NCC):
            pr = tp % 2
            if ("st", pr) in pv:
                b.wait("sync", b.s_vec, pv[("st", pr)])
                b.wait("sync", b.s_sca, pv[("w", pr)])
            v_c = b.dma("sync", cumt[pr], cum_d[tp * 128:(tp + 1) * 128, :])
            v_x = b.dma("sync", dtxt[pr], dtxp_d[tp * 128:(tp + 1) * 128, :])
            b.waitt("sca", v_c)
            b.waitt("vec", v_x)
            for n in range(NST):
                col = tp * NST + n
                npr = n % 2
                if ("stt", npr) in pv:
                    b.wait("sca", b.s_vec, pv[("stt", npr)])
                v_w = b.op("sca", lambda e, col=col, npr=npr, pr=pr: e.activation(
                    wsuf[npr], cumt[pr], AF.Exp, scale=negA_s[:, col:col + 1],
                    bias=biasA_s[:, col:col + 1]), inc=True)
                pv[("w", pr)] = v_w
                v_db = b.op("vec", lambda e, n=n, npr=npr, pr=pr: e.tensor_tensor(
                    dbxp[npr], dtxt[pr], brep[:, n * TOK:(n + 1) * TOK], ALU.mult),
                    inc=True)
                b.wait("vec", b.s_sca, v_w)
                v_st = b.op("vec", lambda e, col=col, npr=npr: e.scalar_tensor_tensor(
                    scrx, wsuf[npr], 1.0, dbxp[npr], ALU.mult, ALU.mult,
                    accum_out=state_s[:, col:col + 1]), inc=True)
                pv[("stt", npr)] = v_st
                pv[("st", pr)] = v_st
        b.op("vec", lambda e: e.tensor_scalar_mul(state_s[:], state_s[:], par_s[:]),
             inc=True)
        b.fence()

    if phases >= 5:
        b.scope("suffix")
        suffix_state()

    def scan_phase():
        brep = BUF1[:, 0:8192].bitcast(BF16)
        crep = BUF1[:, 8192:16384].bitcast(BF16)
        dbx = [BUF2[:, 0:4096].bitcast(BF16),
               BUF2[:, 4096:8192].bitcast(BF16)]
        yt = BUF3[:].bitcast(BF16)
        dA = [BUF2[:, 8192 + i * 1024: 8192 + (i + 1) * 1024] for i in range(4)]
        dtt = [BUF2[:, 12288 + i * 512: 12288 + (i + 1) * 512].bitcast(BF16)
               for i in range(2)]
        dtxt = [BUF2[:, 13312 + i * 512: 13312 + (i + 1) * 512].bitcast(BF16)
                for i in range(2)]
        xcg = [BUF2[:, 14336 + i * 512: 14336 + (i + 1) * 512].bitcast(BF16)
               for i in range(2)]
        ztt = BUF2[:, 15360:15872].bitcast(BF16)
        szt = BUF2[:, 15872:16384].bitcast(BF16)
        yf = yt[:, 0:1024]
        ygo = yt[:, 1024:2048]
        prev = {}
        for hf in range(2):
            sl = slice(hf * 1024, (hf + 1) * 1024)
            b.vbar("gp")
            v_br = b.dma("gp", brep.rearrange("p (s l) -> p s l", s=NST),
                         Bn_d[:, sl].unsqueeze(0).broadcast_to([128, 16, 1024]))
            v_cr = b.dma("gp", crep.rearrange("p (s l) -> p s l", s=NST),
                         Cn_d[:, sl].unsqueeze(0).broadcast_to([128, 16, 1024]))
            for tp in range(NCC):
                pr = tp % 2
                rows = slice(tp * 128, (tp + 1) * 128)
                if ("sz", pr) in prev:
                    b.wait("sync", b.s_sca, prev[("sz", pr)])
                    b.wait("sync", b.s_vec, prev[("yg", pr)])
                    b.wait("sync", b.s_vec, prev[("db", pr)])
                v_dt = b.dma("sync", dtt[pr], dt_d[rows, sl])
                v_dx = b.dma("sync", dtxt[pr], dtx_d[rows, sl])
                v_xc = b.dma("sync", xcg[pr], xc_d[rows, sl])
                if ("sz", 1 - pr) in prev:
                    b.wait("sync", b.s_sca, prev[("sz", 1 - pr)])
                v_zt = b.dma("sync", ztt, z_d[rows, sl])
                b.waitt("vec", v_dx)
                if tp == 0:
                    b.waitt("vec", v_br)
                    b.waitt("vec", v_cr)
                for dh in range(2):
                    v_db = b.op("vec", lambda e, pr=pr, dh=dh: e.tensor_tensor(
                        dbx[dh].rearrange("p (s l) -> p s l", s=8),
                        dtxt[pr].unsqueeze(1).broadcast_to([128, 8, 1024]),
                        brep.rearrange("p (s l) -> p s l", s=NST)[:, dh * 8:(dh + 1) * 8, :],
                        ALU.mult), inc=True)
                prev[("db", pr)] = v_db
                b.waitt("sca", v_dt)
                for n in range(NST):
                    col = tp * NST + n
                    dh = n // 8
                    if ("scan", col % 4) in prev:
                        b.wait("sca", b.s_vec, prev[("scan", col % 4)])
                    v_a = b.op("sca", lambda e, pr=pr, col=col: e.activation(
                        dA[col % 4], dtt[pr], AF.Exp, scale=A_s[:, col:col + 1]),
                        inc=True)
                    if n == 1 and ("ygst", 0) in prev:
                        b.waitt("vec", prev[("ygst", 0)])
                    b.wait("vec", b.s_sca, v_a)
                    v_s = b.op("vec", lambda e, n=n, col=col, dh=dh: e.tensor_tensor_scan(
                        yt[:, n * 1024:(n + 1) * 1024], dA[col % 4],
                        dbx[dh][:, (n % 8) * 1024:(n % 8 + 1) * 1024],
                        state_s[:, col:col + 1], ALU.mult, ALU.add), inc=True)
                    prev[("scan", col % 4)] = v_s
                    if n % 8 == 7:
                        prev[("scanh", dh)] = v_s
                prev[("scans", 0)] = v_s
                if hf == 0:
                    b.op("vec", lambda e, tp=tp: e.tensor_copy(
                        state_s[:].rearrange("p (a s) -> p a s", s=NST)[:, tp, :],
                        yt[:].rearrange("p (s l) -> p s l", s=NST)[:, :, 1023]))
                b.op("vec", lambda e: e.tensor_tensor(yt[:], yt[:], crep[:], ALU.mult))
                b.op("vec", lambda e: e.tensor_tensor(
                    yt[:, 0:8192], yt[:, 0:8192], yt[:, 8192:16384], ALU.add))
                b.op("vec", lambda e: e.tensor_tensor(
                    yt[:, 0:4096], yt[:, 0:4096], yt[:, 4096:8192], ALU.add))
                b.op("vec", lambda e: e.tensor_tensor(
                    yt[:, 0:2048], yt[:, 0:2048], yt[:, 2048:4096], ALU.add))
                v_tr = b.op("vec", lambda e: e.tensor_tensor(
                    yf, yt[:, 0:1024], yt[:, 1024:2048], ALU.add), inc=True)
                b.waitt("sca", v_zt)
                if ("yg", 1 - pr) in prev:
                    b.wait("sca", b.s_vec, prev[("yg", 1 - pr)])
                v_sz = b.op("sca", lambda e: e.activation(szt, ztt, AF.Silu),
                            inc=True)
                prev[("sz", pr)] = v_sz
                b.waitt("vec", v_xc)
                b.wait("vec", b.s_sca, v_sz)
                b.op("vec", lambda e, pr=pr, tp=tp: e.scalar_tensor_tensor(
                    yf, xcg[pr], Dpar_s[:, tp:tp + 1], yf, ALU.mult, ALU.add))
                v_yg = b.op("vec", lambda e: e.tensor_tensor(ygo, yf, szt,
                                                             ALU.mult), inc=True)
                prev[("yg", pr)] = v_yg
                b.wait("sync", b.s_vec, v_yg)
                prev[("ygst", 0)] = b.store(yg_d[rows, sl], ygo)
        b.fence()

    if phases >= 6:
        b.scope("scan")
        scan_phase()

    def outproj_phase():
        ygslab = BUF1[:].bitcast(BF16)
        woA = [BUF3[:, 0:4096].bitcast(BF16), BUF3[:, 4096:8192].bitcast(BF16)]
        woB = [BUF2[:, 2048:6144].bitcast(BF16), BUF2[:, 6144:10240].bitcast(BF16)]
        x1t = BUF2[:, 0:2048]
        v_st2 = None
        pvo = {}
        for lb in range(4):
            sl = slice(lb * 512, (lb + 1) * 512)
            b.stbar("sync")
            v_slab = b.dma("sync", ygslab[:, 0:NCC * 512]
                           .rearrange("p (a l) -> p a l", a=NCC),
                           yg_d[:, sl].rearrange("(a p) l -> p a l", p=128))
            b.waitt("ten", v_slab)
            for hp in range(NHC // 2):
                wo = woA if hp % 2 == 0 else woB
                wtoks = []
                for h2 in range(2):
                    hc = hp * 2 + h2
                    if pvo.get("mm2"):
                        b.wait("gp", b.s_mm, pvo["mm2"])
                    wtoks.append(b.dma("gp", wo[h2], WoutT[:, hc, :]))
                v_w0, v_w1 = wtoks
                b.waitt("ten", v_w0)
                b.waitt("ten", v_w1)
                b.vbar("ten")

                def mm_o(e, hp=hp, wo=wo):
                    last = None
                    for cg in range(NCC):
                        for h2 in range(2):
                            last = e.matmul([pA0, pA1][h2][:, 0:512],
                                            wo[h2].rearrange("p (a c) -> p a c", a=NCC)
                                            [:, cg, :],
                                            ygslab[:, cg * 512:(cg + 1) * 512],
                                            start=(cg == 0), stop=(cg == NCC - 1))
                    return last
                v_m = b.op("ten", mm_o, inc=True)
                pvo["mm2"] = pvo.get("mm1")
                pvo["mm1"] = v_m
                b.vbar("sync")
                b.waitt("sync", v_st2)
                v_x1 = b.dma("sync", x1t[:, 0:1024].rearrange("p (a l) -> p a l", a=2),
                             x1_d[:, sl].rearrange("(a p) l -> p a l", p=128)
                             [:, hp * 2:hp * 2 + 2, :])
                b.wait("vec", b.s_mm, v_m)
                b.waitt("vec", v_x1)
                v_x2 = None
                for h2 in range(2):
                    v_x2 = b.op("vec", lambda e, h2=h2: e.scalar_tensor_tensor(
                        x1t[:, h2 * 512:(h2 + 1) * 512], [pA0, pA1][h2][:, 0:512], 1.0,
                        x1t[:, h2 * 512:(h2 + 1) * 512], ALU.mult, ALU.add), inc=True)
                b.wait("sync", b.s_vec, v_x2)
                v_st2 = b.store(x2_d[:, sl].rearrange("(a p) l -> p a l", p=128)
                                [:, hp * 2:hp * 2 + 2, :],
                                x1t[:, 0:1024].rearrange("p (a l) -> p a l", a=2))
        b.fence()

    if phases >= 7:
        b.scope("outproj")
        outproj_phase()

    vla2 = b.dma("gp", BUF3[:].bitcast(BF16), lAT)
    b.waitt("ten", vla2)
    if phases >= 8:
        b.scope("D")
        phaseA(x2_d, out, None, BUF3[:].bitcast(BF16))

    if probe is not None:
        b.fence()
        probes = dict(x1=x1_d, ln1=ln1_d, ln1p=ln1p_d, xc=xc_d, z=z_d, dt=dt_d,
                      dtx=dtx_d, dtp=dtp_d, dtxp=dtxp_d, xcp=xcp_d, cum=cum_d,
                      yg=yg_d, x2=x2_d, xh=xh_d, w8=w8_d, rows=rows_d)
        pa = probes[probe]
        rows = min(pa.shape[0], H)
        cols = min(pa.shape[1], TOK)
        b.dma("gp", out[0:rows, 0:cols], pa[0:rows, 0:cols])
        if probe in ("cum",):
            b.dma("gp", out[rows: 2 * rows if 2 * rows <= H else H, 0:cols],
                  pa[rows:2 * rows, 0:cols]) if pa.shape[0] > rows and 2 * rows <= H else None
        b.wait("gp", b.s_gld, b.val(b.s_gld))

    b.wait("sync", b.s_st, b.val(b.s_st))

    def run_stream(e, lst):
        opened = None
        for fn in lst:
            if isinstance(fn, tuple) and fn[0] == "SCOPE":
                if opened:
                    nc.pop_named_scope(opened)
                nc.push_named_scope(fn[1])
                opened = fn[1]
            else:
                fn(e)
        if opened:
            nc.pop_named_scope(opened)

    with nc.Block() as block:
        @block.sync
        def _(e):
            run_stream(e, b.E["sync"])

        @block.gpsimd
        def _(e):
            run_stream(e, b.E["gp"])

        @block.vector
        def _(e):
            run_stream(e, b.E["vec"])

        @block.scalar
        def _(e):
            run_stream(e, b.E["sca"])

        @block.tensor
        def _(e):
            run_stream(e, b.E["ten"])

    b.es.close()
    return nc


_NC_CACHE = None


def kernel(L0_attn_hidden_states, L0_attn_in_proj_w, L0_attn_conv_w,
           L0_attn_conv_b, L0_attn_x_proj_w, L0_attn_dt_proj_w,
           L0_attn_dt_bias, L0_attn_A_log, L0_attn_D_param,
           L0_attn_out_proj_w, L0_attn_dt_ln_w, L0_attn_B_ln_w,
           L0_attn_C_ln_w, L0_ffn_gate_w, L0_ffn_lora_A, L0_ffn_lora_B):
    global _NC_CACHE
    x = np.asarray(L0_attn_hidden_states, dtype=np.float32)
    Win0 = np.asarray(L0_attn_in_proj_w, np.float32)
    Win = np.ascontiguousarray(
        Win0.reshape(NHC, 128, 2 * NCC, 128).transpose(1, 2, 0, 3)
        .reshape(128, 2 * NCC, NHC * 128))
    Wout0 = np.ascontiguousarray(np.asarray(L0_attn_out_proj_w, np.float32).T)
    WoutT = np.ascontiguousarray(
        Wout0.reshape(NCC, 128, NHC, 128).transpose(1, 2, 0, 3)
        .reshape(128, NHC, NCC * 128))
    xpwT = np.ascontiguousarray(np.asarray(L0_attn_x_proj_w, np.float32).T)
    dtw0 = np.ascontiguousarray(np.asarray(L0_attn_dt_proj_w, np.float32).T)
    dtwT = np.ascontiguousarray(
        dtw0.reshape(2, 128, NCC, 128).transpose(1, 0, 2, 3).reshape(128, 2 * NCC * 128))
    lA0 = np.transpose(np.asarray(L0_ffn_lora_A, np.float32), (0, 2, 1))
    lAT = np.ascontiguousarray(
        lA0.reshape(NE, NHC, 128, ER).transpose(2, 0, 1, 3).reshape(128, NE * NHC * ER))
    lBT = np.ascontiguousarray(np.transpose(np.asarray(L0_ffn_lora_B, np.float32), (0, 2, 1)))
    gateT = np.ascontiguousarray(np.asarray(L0_ffn_gate_w, np.float32).T)
    conv_w = np.asarray(L0_attn_conv_w, np.float32)
    conv_b = np.asarray(L0_attn_conv_b, np.float32)
    dt_bias = np.asarray(L0_attn_dt_bias, np.float32)
    A_log = np.asarray(L0_attn_A_log, np.float32)
    D_param = np.asarray(L0_attn_D_param, np.float32)
    dt_ln_w = np.asarray(L0_attn_dt_ln_w, np.float32)
    B_ln_w = np.asarray(L0_attn_B_ln_w, np.float32)
    C_ln_w = np.asarray(L0_attn_C_ln_w, np.float32)

    convw = np.ascontiguousarray(
        conv_w.reshape(NCC, 128, 4).transpose(1, 0, 2).reshape(128, NCC * 4))
    convb = np.ascontiguousarray(conv_b.reshape(NCC, 128).T)
    dtbias = np.ascontiguousarray(dt_bias.reshape(NCC, 128).T)
    Dparh = np.ascontiguousarray(D_param.reshape(NCC, 128).T)
    AlogT = np.ascontiguousarray(
        A_log.reshape(NCC, 128, NST).transpose(1, 0, 2).reshape(128, NCC * NST))
    dtlnw = np.ascontiguousarray(dt_ln_w.reshape(2, 128).T)
    Blnw = np.ascontiguousarray(B_ln_w.reshape(16, 1))
    Clnw = np.ascontiguousarray(C_ln_w.reshape(16, 1))
    ident8 = np.eye(8, dtype=np.float32)
    idw128 = np.eye(128, dtype=np.float32)
    ones1 = np.ones((128, 1), np.float32)

    if _NC_CACHE is None:
        _NC_CACHE = build()
    nc = _NC_CACHE

    in_maps = []
    for c in range(8):
        bb, half = c // 2, c % 2
        xTb = np.ascontiguousarray(x[bb].T)
        in_maps.append(dict(
            xT=np.ascontiguousarray(xTb[:, half * TOK:(half + 1) * TOK]),
            xTp=np.ascontiguousarray(xTb[:, 0:TOK]),
            parity=np.full((128, 1), float(half), np.float32),
            Win=Win, WoutT=WoutT, xpwT=xpwT, dtwT=dtwT, lAT=lAT, lBT=lBT,
            gateT=gateT, convw=convw, convb=convb, dtbias=dtbias, Dpar=Dparh,
            AlogT=AlogT, dtlnw=dtlnw, Blnw=Blnw, Clnw=Clnw,
            ident8=ident8, idw128=idw128, ones1=ones1,
        ))

    res = run_bass_kernel_spmd(nc, in_maps, core_ids=list(range(8)))
    outarr = np.zeros((4, 4096, 4096), np.float32)
    for c in range(8):
        bb, half = c // 2, c % 2
        outarr[bb, half * TOK:(half + 1) * TOK, :] = res.results[c]["out"].T
    return outarr


# version 19
# speedup vs baseline: 1.2085x; 1.2085x over previous
import contextlib

import numpy as np

import concourse.bass as bass
import concourse.mybir as mybir
from concourse.bass_utils import run_bass_kernel_spmd

F32 = mybir.dt.float32
BF16 = mybir.dt.bfloat16
AF = mybir.ActivationFunctionType
ALU = mybir.AluOpType
AX = mybir.AxisListType

H = 4096
DIN = 8192
NST = 16
RNK = 256
TOK = 2048
NHC = H // 128
NCC = DIN // 128
NE = 8
ER = 64
EPS_LN = 1e-5
EPS_RMS = 1e-6


class B:
    def __init__(self):
        self.nc = bass.Bass("TRN2", target_bir_lowering=False, debug=False)
        self.E = {"sync": [], "gp": [], "vec": [], "sca": [], "ten": []}
        self.cnt = {}
        self.es = contextlib.ExitStack()

    def em(self, eng, fn):
        self.E[eng].append(fn)

    def scope(self, name):
        for eng in self.E:
            self.E[eng].append(("SCOPE", name))

    def inc(self, sem, n):
        k = sem.name
        self.cnt[k] = self.cnt.get(k, 0) + n
        return self.cnt[k]

    def val(self, sem):
        return self.cnt.get(sem.name, 0)

    def wait(self, eng, sem, v):
        if sem.name == "s_ld":
            return self.ldbar(eng)
        if sem.name == "s_gld":
            return self.gldbar(eng)
        if sem.name == "s_st":
            return self.stbar(eng)
        if v > 0:
            self.em(eng, lambda e, s=sem, vv=v: e.wait_ge(s, vv))

    def _ring_dma(self, q, ring, dst, src):
        i = self.rix[ring] % 4
        self.rix[ring] += 1
        sem = self.rings[ring][i]
        prev = self.val(sem)
        if prev > 0:
            self.em(q, lambda e, s=sem, vv=prev: e.wait_ge(s, vv))
        v = self.inc(sem, 16)
        self.em(q, lambda e, d=dst, s=src, sm=sem: e.dma_start(out=d, in_=s).then_inc(sm, 16))
        return (sem, v)

    def dma(self, q, dst, src):
        return self._ring_dma(q, "ld" if q == "sync" else "gld", dst, src)

    def store(self, dst, src):
        return self._ring_dma("sync", "st", dst, src)

    def waitt(self, eng, tok):
        if tok is None:
            return
        sem, v = tok
        self.em(eng, lambda e, s=sem, vv=v: e.wait_ge(s, vv))

    def ringbar(self, eng, ring):
        for sem in self.rings[ring]:
            v = self.val(sem)
            if v > 0:
                self.em(eng, lambda e, s=sem, vv=v: e.wait_ge(s, vv))

    def ldbar(self, eng):
        self.ringbar(eng, "ld")

    def gldbar(self, eng):
        self.ringbar(eng, "gld")

    def op(self, eng, fn, inc=True):
        sem = self.esem[eng]
        prev = self.val(sem)
        if eng in ("vec", "sca", "gp") and prev > 0:
            self.em(eng, lambda e, s=sem, vv=prev: e.wait_ge(s, vv))
        v = self.inc(sem, 1)
        self.em(eng, lambda e, f=fn, s=sem: f(e).then_inc(s, 1))
        return v

    def vbar(self, eng):
        self.wait(eng, self.s_vec, self.val(self.s_vec))

    def sbar(self, eng):
        self.wait(eng, self.s_sca, self.val(self.s_sca))

    def tbar(self, eng):
        self.wait(eng, self.s_mm, self.val(self.s_mm))

    def stbar(self, eng):
        self.ringbar(eng, "st")

    def fence(self):
        self.ldbar("sync")
        self.stbar("sync")
        self.gldbar("gp")
        tot = self.inc(self.s_f, 5)
        for eng in ("sync", "gp", "vec", "sca", "ten"):
            self.em(eng, lambda e: e.sem_inc(self.s_f, 1))
        for eng in ("sync", "gp", "vec", "sca", "ten"):
            self.wait(eng, self.s_f, tot)

    def sb(self, name, shape, dtype):
        return self.es.enter_context(self.nc.sbuf_tensor(name, list(shape), dtype))

    def psum(self, name, shape):
        return self.es.enter_context(self.nc.psum_tensor(name, list(shape), F32))

    def sem(self, name):
        return self.es.enter_context(self.nc.semaphore(name))

    def din(self, name, shape):
        return self.nc.dram_tensor(name, list(shape), F32, kind="ExternalInput").ap()

    def scr(self, name, shape, dtype):
        return self.nc.dram_tensor(name, list(shape), dtype, kind="Internal").ap()


def build(phases=8, probe=None, no_ffn=False):
    b = B()
    nc = b.nc

    xT = b.din("xT", [H, TOK])
    xTp = b.din("xTp", [H, TOK])
    parity = b.din("parity", [128, 1])
    Win = b.din("Win", [128, 2 * NCC, NHC * 128])
    WoutT = b.din("WoutT", [128, NHC, NCC * 128])
    xpwT = b.din("xpwT", [DIN, 288])
    dtwT = b.din("dtwT", [128, 2 * NCC * 128])
    lAT = b.din("lAT", [128, NE * NHC * ER])
    lBT = b.din("lBT", [NE, ER, H])
    gateT = b.din("gateT", [H, NE])
    convw = b.din("convw", [128, NCC * 4])
    convb = b.din("convb", [128, NCC])
    dtbias = b.din("dtbias", [128, NCC])
    Dpar = b.din("Dpar", [128, NCC])
    AlogT = b.din("AlogT", [128, NCC * NST])
    dtlnw = b.din("dtlnw", [128, 2])
    Blnw = b.din("Blnw", [16, 1])
    Clnw = b.din("Clnw", [16, 1])
    ident8 = b.din("ident8", [8, 8])
    idw128 = b.din("idw128", [128, 128])
    ones1 = b.din("ones1", [128, 1])
    out = nc.dram_tensor("out", [H, TOK], F32, kind="ExternalOutput").ap()

    Winb_d = b.scr("Winb_d", [128, 2 * NCC, NHC * 128], BF16)
    Woutb_d = b.scr("Woutb_d", [128, NHC, NCC * 128], BF16)
    x1_d = b.scr("x1_d", [H, TOK], F32)
    ln1_d = b.scr("ln1_d", [H, TOK], BF16)
    ln1p_d = b.scr("ln1p_d", [H, TOK], BF16)
    xh_d = b.scr("xh_d", [DIN, 1024], BF16)
    xc_d = b.scr("xc_d", [DIN, TOK], BF16)
    xcp_d = b.scr("xcp_d", [DIN, TOK], BF16)
    z_d = b.scr("z_d", [DIN, TOK], BF16)
    dt_d = b.scr("dt_d", [DIN, TOK], BF16)
    dtx_d = b.scr("dtx_d", [DIN, TOK], BF16)
    dtp_d = b.scr("dtp_d", [DIN, TOK], BF16)
    dtxp_d = b.scr("dtxp_d", [DIN, TOK], BF16)
    Bn_d = b.scr("Bn_d", [16, TOK], F32)
    Cn_d = b.scr("Cn_d", [16, TOK], F32)
    Bnp_d = b.scr("Bnp_d", [16, TOK], F32)
    cum_d = b.scr("cum_d", [DIN, TOK], F32)
    yg_d = b.scr("yg_d", [DIN, TOK], BF16)
    rows_d = b.scr("rows_d", [8, TOK], F32)
    w8_d = b.scr("w8_d", [8, TOK], BF16)
    x2_d = b.scr("x2_d", [H, TOK], F32)

    b.s_ld = b.sem("s_ld")
    b.s_gld = b.sem("s_gld")
    b.s_st = b.sem("s_st")
    b.s_mm = b.sem("s_mm")
    b.s_vec = b.sem("s_vec")
    b.s_sca = b.sem("s_sca")
    b.s_gp = b.sem("s_gp")
    b.s_f = b.sem("s_f")
    b.esem = {"vec": b.s_vec, "sca": b.s_sca, "gp": b.s_gp, "ten": b.s_mm}
    b.rings = {
        "ld": [b.sem(f"r_ld{i}") for i in range(4)],
        "gld": [b.sem(f"r_gld{i}") for i in range(4)],
        "st": [b.sem(f"r_st{i}") for i in range(4)],
    }
    b.rix = {"ld": 0, "gld": 0, "st": 0}

    convw_s = b.sb("convw_s", [128, NCC * 4], F32)
    convb_s = b.sb("convb_s", [128, NCC], F32)
    dtbias_s = b.sb("dtbias_s", [128, NCC], F32)
    Dpar_s = b.sb("Dpar_s", [128, NCC], F32)
    A_s = b.sb("A_s", [128, NCC * NST], F32)
    cumend_s = b.sb("cumend_s", [128, NCC], F32)
    dtlnw_s = b.sb("dtlnw_s", [128, 2], F32)
    Blnw_s = b.sb("Blnw_s", [16, 1], F32)
    Clnw_s = b.sb("Clnw_s", [16, 1], F32)
    par_s = b.sb("par_s", [128, 1], F32)
    epsln_s = b.sb("epsln_s", [128, 1], F32)
    id8_s = b.sb("id8_s", [8, 8], BF16)
    idw_s = b.sb("idw_s", [128, 128], BF16)
    ones_s = b.sb("ones_s", [128, 1], BF16)
    onesf_s = b.sb("onesf_s", [128, 1], F32)
    gate_s = b.sb("gate_s", [128, NHC * NE], BF16)
    halo_s = b.sb("halo_s", [128, NCC * 3], F32)
    state_s = b.sb("state_s", [128, NCC * NST], F32)
    r_m = b.sb("r_m", [1, 512], F32)
    r_mq = b.sb("r_mq", [1, 512], F32)
    ab_a = b.sb("ab_a", [128, 512], F32)
    ab_b = b.sb("ab_b", [128, 512], F32)
    r_a = ab_a[0:1, :]
    r_b = ab_b[0:1, :]
    g_sb = b.sb("g_sb", [8, 512], BF16)
    p8_t = b.sb("p8_t", [128, 32], F32)
    tcol = [b.sb(f"tcol{i}", [128, 1], F32) for i in range(12)]
    scol = [b.sb(f"scol{i}", [128, 1], F32) for i in range(8)]
    tq = b.sb("tq", [128, 80], F32)
    tmpc = b.sb("tmpc", [128, 1], F32)
    m1_t = b.sb("m1_t", [128, 4], F32)
    m2_t = b.sb("m2_t", [128, 4], F32)
    nm1_t = b.sb("nm1_t", [128, 4], F32)
    e8_t = b.sb("e8_t", [128, 32], F32)
    msk_t = b.sb("msk_t", [128, 32], F32)
    ssum_t = b.sb("ssum_t", [128, 4], F32)
    srec_t = b.sb("srec_t", [128, 4], F32)
    w8_t = b.sb("w8_t", [128, 32], BF16)
    wbc_t = b.sb("wbc_t", [64, NE * 512], BF16)
    us_t = b.sb("us_t", [64, NE * 512], BF16)
    lb_st = [b.sb("lb_st0", [64, 8 * 512], BF16)]

    BUF1 = b.sb("BUF1", [128, 16384], F32)
    BUF2 = b.sb("BUF2", [128, 16384], F32)
    BUF3 = b.sb("BUF3", [128, 8192], F32)

    pA0 = b.psum("pA0", [128, 1024])
    pA1 = b.psum("pA1", [128, 1024])
    pB = b.psum("pB", [128, 1024])
    pC = b.psum("pC", [128, 512])
    pD = b.psum("pD", [128, 512])

    b.dma("sync", convw_s[:], convw)
    b.dma("sync", convb_s[:], convb)
    b.dma("sync", dtbias_s[:], dtbias)
    b.dma("sync", Dpar_s[:], Dpar)
    b.dma("sync", A_s[:], AlogT)
    b.dma("sync", dtlnw_s[:], dtlnw)
    b.dma("sync", Blnw_s[:], Blnw)
    b.dma("sync", Clnw_s[:], Clnw)
    b.dma("sync", par_s[:], parity)
    v0 = b.dma("sync", onesf_s[:], ones1)
    b.dma("gp", id8_s[:], ident8)
    b.dma("gp", idw_s[:], idw128)
    b.dma("gp", ones_s[:], ones1)
    b.dma("gp", gate_s[:].rearrange("p (a e) -> p a e", a=NHC),
          gateT.rearrange("(a p) e -> p a e", p=128))
    b.op("vec", lambda e: e.memset(epsln_s[:], EPS_LN))
    b.wait("sca", b.s_ld, v0)
    b.op("sca", lambda e: e.activation(A_s[:], A_s[:], AF.Exp, scale=1.0))
    vA = b.op("sca", lambda e: e.mul(A_s[:], A_s[:], -1.0), inc=True)
    b.fence()

    def emit_ln(xin_t, lnout_t, xsq, lt):
        v_sq = b.op("sca", lambda e: e.activation(xsq, xin_t, AF.Square), inc=True)
        b.vbar("ten")
        b.wait("ten", b.s_sca, v_sq)

        def mms(e):
            for hc in range(NHC):
                e.matmul(pC[0:1, :], onesf_s[:], xin_t[:, hc * 512:(hc + 1) * 512],
                         start=(hc == 0), stop=(hc == NHC - 1))
            for hc in range(NHC - 1):
                e.matmul(pD[0:1, :], ones_s[:], xsq[:, hc * 512:(hc + 1) * 512],
                         start=(hc == 0), stop=False)
            return e.matmul(pD[0:1, :], ones_s[:],
                            xsq[:, (NHC - 1) * 512:NHC * 512], start=False, stop=True)
        v_mm = b.op("ten", mms, inc=True)
        b.wait("vec", b.s_mm, v_mm)
        b.op("vec", lambda e: e.tensor_scalar_mul(r_m[:], pC[0:1, :], 1.0 / H))
        b.op("vec", lambda e: e.tensor_scalar_mul(r_mq[:], pD[0:1, :], 1.0 / H))
        b.op("vec", lambda e: e.tensor_tensor(r_a, r_m[:], r_m[:], ALU.mult))
        v_v = b.op("vec", lambda e: e.tensor_tensor(r_mq[:], r_mq[:], r_a,
                                                    ALU.subtract), inc=True)
        b.wait("sca", b.s_vec, v_v)
        v_sd = b.op("sca", lambda e: e.activation(r_mq[:], r_mq[:], AF.Sqrt,
                                                  bias=epsln_s[0:1, :]), inc=True)
        b.wait("vec", b.s_sca, v_sd)
        b.stbar("vec")
        b.op("vec", lambda e: e.reciprocal(r_a, r_mq[:]))
        v_ab = b.op("vec", lambda e: e.scalar_tensor_tensor(
            r_b, r_m[:], -1.0, r_a, ALU.mult, ALU.mult), inc=True)
        b.wait("sync", b.s_vec, v_ab)
        b.store(rows_d[0:1, lt * 512:(lt + 1) * 512], r_a)
        v_s = b.store(rows_d[1:2, lt * 512:(lt + 1) * 512], r_b)
        b.wait("sync", b.s_st, v_s)
        v_a = b.dma("sync", ab_a[:], rows_d[0:1, lt * 512:(lt + 1) * 512].partition_broadcast(128))
        v_b = b.dma("sync", ab_b[:], rows_d[1:2, lt * 512:(lt + 1) * 512].partition_broadcast(128))
        b.waitt("vec", v_a)
        b.waitt("vec", v_b)
        a_bc = ab_a[:].unsqueeze(1).broadcast_to([128, NHC, 512])
        b_bc = ab_b[:].unsqueeze(1).broadcast_to([128, NHC, 512])
        x3 = xin_t.rearrange("p (a l) -> p a l", a=NHC)
        q3 = xsq.rearrange("p (a l) -> p a l", a=NHC)
        l3 = lnout_t.rearrange("p (a l) -> p a l", a=NHC)
        b.op("vec", lambda e: e.tensor_tensor(q3, x3, a_bc, ALU.mult))
        return b.op("vec", lambda e: e.tensor_tensor(l3, q3, b_bc, ALU.add), inc=True)

    def emit_ffn(x_t, ln_t, lt, lA_v):
        b.vbar("ten")

        def mm_g(e):
            for hc in range(NHC - 1):
                e.matmul(pC[0:8, :], gate_s[:, hc * 8:(hc + 1) * 8],
                         ln_t[:, hc * 512:(hc + 1) * 512], start=(hc == 0), stop=False)
            return e.matmul(pC[0:8, :], gate_s[:, (NHC - 1) * 8:NHC * 8],
                            ln_t[:, (NHC - 1) * 512:NHC * 512], start=False, stop=True)
        v_g = b.op("ten", mm_g, inc=True)
        b.wait("sca", b.s_mm, v_g)
        v_ge = b.op("sca", lambda e: e.copy(g_sb[:], pC[0:8, :]), inc=True)
        b.wait("ten", b.s_sca, v_ge)
        for s4 in range(4):
            b.vbar("ten")
            v_t8 = b.op("ten", lambda e, s=s4: e.matmul(
                pD[:, 0:8], g_sb[:, s * 128:(s + 1) * 128], id8_s[:],
                start=True, stop=True), inc=True)
            b.wait("vec", b.s_mm, v_t8)
            b.op("vec", lambda e, s=s4: e.tensor_copy(p8_t[:, s * 8:(s + 1) * 8], pD[:, 0:8]),
                 inc=True)
        P = p8_t.rearrange("p (s e) -> p s e", s=4)

        def Pc(i):
            return P[:, :, i:i + 1]

        def tv(k):
            return tq[:, k * 4:(k + 1) * 4].rearrange("p (s o) -> p s o", o=1)

        V = lambda f: b.op("vec", f)
        for i in range(4):
            V(lambda e, i=i: e.tensor_tensor(tv(i), Pc(2 * i), Pc(2 * i + 1), ALU.max))
        for i in range(4):
            V(lambda e, i=i: e.tensor_tensor(tv(4 + i), Pc(2 * i), Pc(2 * i + 1), ALU.min))
        q = [tv(i) for i in range(4)]
        n = [tv(4 + i) for i in range(4)]
        r0, r1, t10, g, s0, s1, s2, g01, w4a, g23, w4b, tmp = [tv(8 + i) for i in range(12)]
        m1v = m1_t[:].rearrange("p (s o) -> p s o", o=1)
        m2v = m2_t[:].rearrange("p (s o) -> p s o", o=1)
        V(lambda e: e.tensor_tensor(r0, q[0], q[1], ALU.max))
        V(lambda e: e.tensor_tensor(r1, q[2], q[3], ALU.max))
        V(lambda e: e.tensor_tensor(m1v, r0, r1, ALU.max))
        V(lambda e: e.tensor_tensor(t10, r0, r1, ALU.min))
        V(lambda e: e.tensor_tensor(g, r0, r1, ALU.is_ge))
        V(lambda e: e.tensor_tensor(s0, q[0], q[1], ALU.min))
        V(lambda e: e.tensor_tensor(s1, q[2], q[3], ALU.min))
        V(lambda e: e.tensor_tensor(tmp, s0, s1, ALU.subtract))
        V(lambda e: e.tensor_tensor(tmp, g, tmp, ALU.mult))
        V(lambda e: e.tensor_tensor(s2, s1, tmp, ALU.add))
        V(lambda e: e.tensor_tensor(g01, q[0], q[1], ALU.is_ge))
        V(lambda e: e.tensor_tensor(tmp, n[0], n[1], ALU.subtract))
        V(lambda e: e.tensor_tensor(tmp, g01, tmp, ALU.mult))
        V(lambda e: e.tensor_tensor(w4a, n[1], tmp, ALU.add))
        V(lambda e: e.tensor_tensor(g23, q[2], q[3], ALU.is_ge))
        V(lambda e: e.tensor_tensor(tmp, n[2], n[3], ALU.subtract))
        V(lambda e: e.tensor_tensor(tmp, g23, tmp, ALU.mult))
        V(lambda e: e.tensor_tensor(w4b, n[3], tmp, ALU.add))
        V(lambda e: e.tensor_tensor(tmp, w4a, w4b, ALU.subtract))
        V(lambda e: e.tensor_tensor(tmp, g, tmp, ALU.mult))
        V(lambda e: e.tensor_tensor(w4b, w4b, tmp, ALU.add))
        V(lambda e: e.tensor_tensor(t10, t10, s2, ALU.max))
        V(lambda e: e.tensor_tensor(m2v, t10, w4b, ALU.max))
        v_nm = b.op("vec", lambda e: e.tensor_scalar_mul(nm1_t[:], m1_t[:], -1.0), inc=True)
        b.wait("sca", b.s_vec, v_nm)
        v_e8 = None
        for s4 in range(4):
            v_e8 = b.op("sca", lambda e, s=s4: e.activation(
                e8_t[:, s * 8:(s + 1) * 8], p8_t[:, s * 8:(s + 1) * 8], AF.Exp,
                bias=nm1_t[:, s:s + 1]), inc=True)
        b.wait("vec", b.s_sca, v_e8)
        M3 = msk_t.rearrange("p (s e) -> p s e", s=4)
        E3 = e8_t.rearrange("p (s e) -> p s e", s=4)
        W3 = w8_t.rearrange("p (s e) -> p s e", s=4)
        V(lambda e: e.tensor_tensor(
            M3, P, m2_t[:].unsqueeze(2).broadcast_to([128, 4, 8]), ALU.is_ge))
        V(lambda e: e.tensor_tensor(M3, M3, E3, ALU.mult))
        for s4 in range(4):
            b.op("vec", lambda e, s=s4: e.reduce_sum(
                ssum_t[:, s:s + 1], msk_t[:, s * 8:(s + 1) * 8], axis=AX.X))
        b.op("vec", lambda e: e.reciprocal(srec_t[:], ssum_t[:]))
        v_w8 = b.op("vec", lambda e: e.tensor_tensor(
            W3, M3, srec_t[:].unsqueeze(2).broadcast_to([128, 4, 8]), ALU.mult), inc=True)
        b.wait("ten", b.s_vec, v_w8)
        b.sbar("ten")
        v_wT = None
        for s4 in range(4):
            v_wT = b.op("ten", lambda e, s=s4: e.matmul(
                pC[0:8, s * 128:(s + 1) * 128], w8_t[:, s * 8:(s + 1) * 8], idw_s[:],
                start=True, stop=True), inc=True)
        b.wait("sca", b.s_mm, v_wT)
        v_we = b.op("sca", lambda e: e.copy(g_sb[:], pC[0:8, :]), inc=True)
        b.wait("sync", b.s_sca, v_we)
        v_ws = b.store(w8_d[:, lt * 512:(lt + 1) * 512], g_sb[:])
        b.wait("sync", b.s_st, v_ws)
        v_wb = b.dma("sync", wbc_t[:].rearrange("p (e l) -> p e l", e=NE),
                     w8_d[:, lt * 512:(lt + 1) * 512].unsqueeze(0)
                     .broadcast_to([64, 8, 512]))
        b.wait("ten", b.s_sca, v_we)
        v_us = None
        for ei in range(NE):
            def mm_d(e, ei=ei):
                for hc in range(NHC - 1):
                    e.matmul(pC[0:64, :], lA_v[:, (ei * NHC + hc) * ER:(ei * NHC + hc + 1) * ER],
                             ln_t[:, hc * 512:(hc + 1) * 512], start=(hc == 0), stop=False)
                return e.matmul(pC[0:64, :],
                                lA_v[:, (ei * NHC + NHC - 1) * ER:(ei * NHC + NHC) * ER],
                                ln_t[:, (NHC - 1) * 512:NHC * 512], start=False, stop=True)
            v_d = b.op("ten", mm_d, inc=True)
            if ei == 0:
                b.waitt("vec", v_wb)
            b.wait("vec", b.s_mm, v_d)
            v_us = b.op("vec", lambda e, ei=ei: e.tensor_tensor(
                us_t[:, ei * 512:(ei + 1) * 512], pC[0:64, :],
                wbc_t[:, ei * 512:(ei + 1) * 512], ALU.mult), inc=True)
            b.wait("ten", b.s_vec, v_us)
        v_last = None
        for hg in range(NHC // 4):
            lbuf = lb_st[0]
            b.tbar("gp")
            v_lb = b.dma("gp", lbuf[:].rearrange("p (e c) -> p e c", e=NE),
                         lBT[:, :, hg * 512:(hg + 1) * 512].rearrange("e p c -> p e c"))
            b.waitt("ten", v_lb)

            def mm_u(e, hg=hg, lbuf=lbuf):
                last = None
                pslots = [pA0[:, 0:512], pA0[:, 512:1024], pA1[:, 0:512], pA1[:, 512:1024]]
                for ei in range(NE):
                    for hh in range(4):
                        last = e.matmul(pslots[hh],
                                        lbuf[:, ei * 512 + hh * 128: ei * 512 + (hh + 1) * 128],
                                        us_t[:, ei * 512:(ei + 1) * 512],
                                        start=(ei == 0), stop=(ei == NE - 1))
                return last
            v_u = b.op("ten", mm_u, inc=True)
            b.wait("vec", b.s_mm, v_u)
            pslots = [pA0[:, 0:512], pA0[:, 512:1024], pA1[:, 0:512], pA1[:, 512:1024]]
            for hh in range(4):
                hc = hg * 4 + hh
                v_last = b.op("vec", lambda e, hh=hh, hc=hc: e.scalar_tensor_tensor(
                    x_t[:, hc * 512:(hc + 1) * 512], pslots[hh], 0.5,
                    x_t[:, hc * 512:(hc + 1) * 512], ALU.mult, ALU.add), inc=True)
            b.wait("ten", b.s_vec, v_last)
        return v_last

    def phaseA(x_src, res_dst, postln_dst, lA_v):
        x_t = BUF1[:]
        ln_t = BUF2[:, 0:8192].bitcast(BF16)
        xsq = BUF2[:, 8192:16384].bitcast(BF16)
        for lt in range(4):
            b.stbar("sync")
            v_x = b.dma("sync", x_t.rearrange("p (a l) -> p a l", a=NHC),
                        x_src[:, lt * 512:(lt + 1) * 512]
                        .rearrange("(a p) l -> p a l", p=128))
            b.waitt("vec", v_x)
            b.waitt("sca", v_x)
            b.waitt("ten", v_x)
            v_ln = emit_ln(x_t, ln_t, xsq, lt)
            b.wait("sca", b.s_vec, v_ln)
            b.wait("ten", b.s_vec, v_ln)
            if no_ffn:
                v_f = v_ln
            else:
                v_f = emit_ffn(x_t, ln_t, lt, lA_v)
            if res_dst is not None:
                b.wait("sync", b.s_vec, v_f)
                b.store(res_dst[:, lt * 512:(lt + 1) * 512]
                        .rearrange("(a p) l -> p a l", p=128),
                        x_t.rearrange("p (a l) -> p a l", a=NHC))
            if postln_dst is not None:
                b.wait("sca", b.s_vec, v_f)
                v_ln2 = emit_ln(x_t, ln_t, xsq, lt)
                b.wait("sync", b.s_vec, v_ln2)
                b.store(postln_dst[:, lt * 512:(lt + 1) * 512]
                        .rearrange("(a p) l -> p a l", p=128),
                        ln_t.rearrange("p (a l) -> p a l", a=NHC))
        b.fence()

    lA_v = BUF3[:].bitcast(BF16)
    vla = b.dma("gp", lA_v, lAT)
    b.waitt("ten", vla)

    if phases >= 1:
        b.scope("A")
        phaseA(xT, x1_d, ln1_d, lA_v)
    if phases >= 2:
        b.scope("Adup")
        phaseA(xTp, None, ln1p_d, lA_v)

    def phB(ln_dram, is_dup):
        lnslab = BUF1[:].bitcast(BF16)
        wb = [BUF3[:, 0:2048].bitcast(BF16), BUF3[:, 2048:4096].bitcast(BF16)]
        xpw_s = BUF3[:, 4096:4672].bitcast(BF16)
        xh_t = BUF2[:, 0:1024].bitcast(BF16)
        xc_t = [BUF2[:, 1024 + i * 1024: 1024 + (i + 1) * 1024].bitcast(BF16)
                for i in range(4)]
        xp_acc = BUF2[:, 5120:9216]
        dte_t = BUF2[:, 9216:11264]
        zt_t = BUF2[:, 9216:10240].bitcast(BF16)
        dt_t = BUF2[:, 11264:12288].bitcast(BF16)
        dtx_t = BUF2[:, 12288:13312].bitcast(BF16)
        xcg = BUF2[:, 13312:14336].bitcast(BF16)
        rst_b = BUF2[:, 2048:4096]
        dtr_t = BUF2[:, 4096:6144].bitcast(BF16)
        xsqB = BUF2[:, 0:2048].bitcast(BF16)
        bn_acc = BUF2[0:16, 12288:14336]
        cn_acc = BUF2[0:16, 14336:16384]

        dt_dst = dtp_d if is_dup else dt_d
        dtx_dst = dtxp_d if is_dup else dtx_d
        xc_dst = xcp_d if is_dup else xc_d
        bn_dst = Bnp_d if is_dup else Bn_d

        b.op("vec", lambda e: e.memset(xp_acc, 0.0))
        b.op("vec", lambda e: e.memset(bn_acc, 0.0))
        v_z = b.op("vec", lambda e: e.memset(cn_acc, 0.0), inc=True)
        b.wait("ten", b.s_vec, v_z)

        n_cc = NCC if is_dup else 2 * NCC
        prevB = {}
        for half in range(2):
            b.stbar("sync")
            v_slab = b.dma("sync", lnslab[:, 0:NHC * 1024]
                           .rearrange("p (a l) -> p a l", a=NHC),
                           ln_dram[:, half * 1024:(half + 1) * 1024]
                           .rearrange("(a p) l -> p a l", p=128))
            b.waitt("ten", v_slab)
            for cc in range(n_cc):
                is_x = cc < NCC
                buf = wb[cc % 2]
                if ("mm", cc % 2) in prevB:
                    b.wait("gp", b.s_mm, prevB[("mm", cc % 2)])
                v_w = b.dma("gp", buf, Win[:, cc, :])
                b.waitt("ten", v_w)
                if ("ev", cc % 2) in prevB:
                    b.wait("ten", b.s_sca, prevB[("ev", cc % 2)])
                pp = pA0 if (cc % 2 == 0) else pA1

                def mm_ip(e, buf=buf, pp=pp):
                    last = None
                    for lq in range(2):
                        for hc in range(NHC):
                            last = e.matmul(pp[:, lq * 512:(lq + 1) * 512],
                                            buf[:, hc * 128:(hc + 1) * 128],
                                            lnslab[:, hc * 1024 + lq * 512:
                                                   hc * 1024 + (lq + 1) * 512],
                                            start=(hc == 0), stop=(hc == NHC - 1))
                    return last
                v_mm = b.op("ten", mm_ip, inc=True)
                prevB[("mm", cc % 2)] = v_mm
                b.wait("sca", b.s_mm, v_mm)
                if is_x and half == 0:
                    b.stbar("sca")
                    v_ev = b.op("sca", lambda e, pp=pp: e.copy(xh_t[:, 0:1024], pp[:]),
                                inc=True)
                    prevB[("ev", cc % 2)] = v_ev
                    b.wait("sync", b.s_sca, v_ev)
                    b.store(xh_d[cc * 128:(cc + 1) * 128, :], xh_t[:, 0:1024])
                elif is_x:
                    b.vbar("sca")
                    v_ev = b.op("sca", lambda e, pp=pp: e.copy(xh_t[:, 1024:2048], pp[:]),
                                inc=True)
                    prevB[("ev", cc % 2)] = v_ev
                    v_x0 = b.dma("sync", xh_t[:, 0:1024], xh_d[cc * 128:(cc + 1) * 128, :])
                    b.wait("vec", b.s_sca, v_ev)
                    b.waitt("vec", v_x0)
                    if is_dup:
                        b.op("vec", lambda e, cc=cc: e.tensor_scalar_mul(
                            halo_s[:, cc * 3:(cc + 1) * 3], xh_t[:, 2045:2048], par_s[:]))
                    b.op("vec", lambda e, cc=cc: e.tensor_scalar(
                        dte_t, xh_t[:, 0:2048], convw_s[:, cc * 4 + 3:cc * 4 + 4],
                        convb_s[:, cc:cc + 1], ALU.mult, ALU.add))
                    for k in range(3):
                        b.op("vec", lambda e, cc=cc, k=k: e.scalar_tensor_tensor(
                            dte_t[:, 3 - k:2048], xh_t[:, 0:2045 + k],
                            convw_s[:, cc * 4 + k:cc * 4 + k + 1],
                            dte_t[:, 3 - k:2048], ALU.mult, ALU.add))
                    if not is_dup:
                        for k in range(3):
                            b.op("vec", lambda e, cc=cc, k=k: e.scalar_tensor_tensor(
                                dte_t[:, 0:3 - k], halo_s[:, cc * 3 + k:(cc + 1) * 3],
                                convw_s[:, cc * 4 + k:cc * 4 + k + 1],
                                dte_t[:, 0:3 - k], ALU.mult, ALU.add))
                    v_cv = b.op("vec", lambda e: e.tensor_copy(tcol[0][:], tcol[0][:]),
                                inc=True)
                    b.wait("sca", b.s_vec, v_cv)
                    b.stbar("sca")
                    v_si = b.op("sca", lambda e, cc=cc: e.activation(
                        xc_t[cc % 4], dte_t, AF.Silu), inc=True)
                    b.wait("sync", b.s_sca, v_si)
                    b.store(xc_dst[cc * 128:(cc + 1) * 128, :], xc_t[cc % 4])
                    if cc % 4 == 3:
                        ccg = cc // 4
                        b.tbar("gp")
                        v_wx = b.dma("gp", xpw_s.rearrange("p (a m) -> p a m", a=4),
                                     xpwT.rearrange("(a p) m -> p a m", p=128)
                                     [:, ccg * 4:(ccg + 1) * 4, :])
                        b.waitt("ten", v_wx)
                        b.wait("ten", b.s_sca, v_si)
                        b.vbar("ten")
                        for lt in range(4):
                            def mm_xp(e, lt=lt):
                                last = None
                                for cci in range(4):
                                    w = xpw_s[:, cci * 288:(cci + 1) * 288]
                                    xcs = xc_t[cci][:, lt * 512:(lt + 1) * 512]
                                    e.matmul(pB[:, 0:512], w[:, 0:128], xcs,
                                             start=(cci == 0), stop=(cci == 3))
                                    e.matmul(pB[:, 512:1024], w[:, 128:256], xcs,
                                             start=(cci == 0), stop=(cci == 3))
                                    e.matmul(pC[0:16, :], w[:, 256:272], xcs,
                                             start=(cci == 0), stop=(cci == 3))
                                    last = e.matmul(pD[0:16, :], w[:, 272:288], xcs,
                                                    start=(cci == 0), stop=(cci == 3))
                                return last
                            v_xp = b.op("ten", mm_xp, inc=True)
                            b.wait("vec", b.s_mm, v_xp)
                            sl = slice(lt * 512, (lt + 1) * 512)
                            b.op("vec", lambda e, sl=sl: e.tensor_tensor(
                                xp_acc[:, sl], xp_acc[:, sl], pB[:, 0:512], ALU.add))
                            b.op("vec", lambda e, sl=sl: e.tensor_tensor(
                                xp_acc[:, 2048 + sl.start:2048 + sl.stop],
                                xp_acc[:, 2048 + sl.start:2048 + sl.stop],
                                pB[:, 512:1024], ALU.add))
                            b.op("vec", lambda e, sl=sl: e.tensor_tensor(
                                bn_acc[:, sl], bn_acc[:, sl], pC[0:16, :], ALU.add))
                            v_fl = b.op("vec", lambda e, sl=sl: e.tensor_tensor(
                                cn_acc[:, sl], cn_acc[:, sl], pD[0:16, :], ALU.add),
                                inc=True)
                            b.wait("ten", b.s_vec, v_fl)
                else:
                    b.stbar("sca")
                    v_ev = b.op("sca", lambda e, pp=pp: e.copy(zt_t[:, 0:1024], pp[:]),
                                inc=True)
                    prevB[("ev", cc % 2)] = v_ev
                    b.wait("sync", b.s_sca, v_ev)
                    b.store(z_d[(cc - NCC) * 128:(cc - NCC + 1) * 128,
                                half * 1024:(half + 1) * 1024],
                            zt_t[:, 0:1024])
        b.fence()

        v_q = b.op("sca", lambda e: e.activation(xsqB, xp_acc, AF.Square), inc=True)
        b.wait("ten", b.s_sca, v_q)
        for lt in range(4):
            b.vbar("ten")

            def mm_ss(e, lt=lt):
                e.matmul(pC[0:1, :], ones_s[:], xsqB[:, lt * 512:(lt + 1) * 512],
                         start=True, stop=False)
                return e.matmul(pC[0:1, :], ones_s[:],
                                xsqB[:, 2048 + lt * 512:2048 + (lt + 1) * 512],
                                start=False, stop=True)
            v_ss = b.op("ten", mm_ss, inc=True)
            b.wait("vec", b.s_mm, v_ss)
            b.op("vec", lambda e: e.tensor_scalar_mul(r_m[:], pC[0:1, :], 1.0 / RNK))
            v_vv = b.op("vec", lambda e: e.tensor_scalar_add(r_m[:], r_m[:], EPS_RMS), inc=True)
            b.wait("sca", b.s_vec, v_vv)
            v_sd = b.op("sca", lambda e: e.activation(r_m[:], r_m[:], AF.Sqrt), inc=True)
            b.wait("vec", b.s_sca, v_sd)
            b.stbar("vec")
            v_rc = b.op("vec", lambda e: e.reciprocal(r_a, r_m[:]), inc=True)
            b.wait("sync", b.s_vec, v_rc)
            v_sr = b.store(rows_d[2:3, lt * 512:(lt + 1) * 512], r_a)
            b.wait("sync", b.s_st, v_sr)
            b.dma("sync", rst_b[:, lt * 512:(lt + 1) * 512],
                  rows_d[2:3, lt * 512:(lt + 1) * 512].partition_broadcast(128))
        b.wait("vec", b.s_ld, b.val(b.s_ld))
        for ch in range(2):
            b.op("vec", lambda e, ch=ch: e.scalar_tensor_tensor(
                dtr_t[:, ch * 2048:(ch + 1) * 2048], xp_acc[:, ch * 2048:(ch + 1) * 2048],
                dtlnw_s[:, ch:ch + 1], rst_b[:, 0:2048], ALU.mult, ALU.mult))
        v_dtr = b.op("vec", lambda e: e.tensor_copy(tcol[0][:], tcol[0][:]), inc=True)

        b.wait("sca", b.s_vec, v_dtr)
        v_qb = b.op("sca", lambda e: e.activation(xsqB[0:16, 0:2048], bn_acc, AF.Square),
                    inc=True)
        v_qc = b.op("sca", lambda e: e.activation(xsqB[0:16, 2048:4096], cn_acc, AF.Square),
                    inc=True)
        b.wait("ten", b.s_sca, v_qc)
        for lt in range(4):
            b.vbar("ten")

            def mm_bc(e, lt=lt):
                e.matmul(pC[0:1, :], ones_s[0:16, :], xsqB[0:16, lt * 512:(lt + 1) * 512],
                         start=True, stop=True)
                return e.matmul(pD[0:1, :], ones_s[0:16, :],
                                xsqB[0:16, 2048 + lt * 512:2048 + (lt + 1) * 512],
                                start=True, stop=True)
            v_bc = b.op("ten", mm_bc, inc=True)
            b.wait("vec", b.s_mm, v_bc)
            b.op("vec", lambda e: e.tensor_scalar_mul(r_m[:], pC[0:1, :], 1.0 / NST))
            b.op("vec", lambda e: e.tensor_scalar_add(r_m[:], r_m[:], EPS_RMS))
            b.op("vec", lambda e: e.tensor_scalar_mul(r_mq[:], pD[0:1, :], 1.0 / NST))
            v_v3 = b.op("vec", lambda e: e.tensor_scalar_add(r_mq[:], r_mq[:], EPS_RMS),
                        inc=True)
            b.wait("sca", b.s_vec, v_v3)
            b.op("sca", lambda e: e.activation(r_m[:], r_m[:], AF.Sqrt))
            v_s2 = b.op("sca", lambda e: e.activation(r_mq[:], r_mq[:], AF.Sqrt), inc=True)
            b.wait("vec", b.s_sca, v_s2)
            b.stbar("vec")
            b.op("vec", lambda e: e.reciprocal(r_a, r_m[:]))
            v_r2 = b.op("vec", lambda e: e.reciprocal(r_b, r_mq[:]), inc=True)
            b.wait("sync", b.s_vec, v_r2)
            b.store(rows_d[3:4, lt * 512:(lt + 1) * 512], r_a)
            v_s4 = b.store(rows_d[4:5, lt * 512:(lt + 1) * 512], r_b)
            b.wait("sync", b.s_st, v_s4)
            b.dma("sync", ab_a[0:16, :],
                  rows_d[3:4, lt * 512:(lt + 1) * 512].partition_broadcast(16))
            v_ab2 = b.dma("sync", ab_b[0:16, :],
                          rows_d[4:5, lt * 512:(lt + 1) * 512].partition_broadcast(16))
            b.wait("vec", b.s_ld, v_ab2)
            b.op("vec", lambda e, lt=lt: e.tensor_tensor(
                bn_acc[:, lt * 512:(lt + 1) * 512], bn_acc[:, lt * 512:(lt + 1) * 512],
                ab_a[0:16, :], ALU.mult))
            v_apl = b.op("vec", lambda e, lt=lt: e.tensor_tensor(
                cn_acc[:, lt * 512:(lt + 1) * 512], cn_acc[:, lt * 512:(lt + 1) * 512],
                ab_b[0:16, :], ALU.mult), inc=True)
            b.wait("sync", b.s_vec, v_apl)
        b.op("vec", lambda e: e.tensor_scalar_mul(bn_acc, bn_acc, Blnw_s[:]))
        v_cn = b.op("vec", lambda e: e.tensor_scalar_mul(cn_acc, cn_acc, Clnw_s[:]),
                    inc=True)
        b.wait("sync", b.s_vec, v_cn)
        b.store(bn_dst[:, :], bn_acc)
        if not is_dup:
            b.store(Cn_d[:, :], cn_acc)

        dtw_v = BUF1[:, 0:8192].bitcast(BF16)
        b.tbar("gp")
        vdw = b.dma("gp", dtw_v, dtwT)
        b.waitt("ten", vdw)
        b.wait("ten", b.s_vec, v_dtr)
        b.stbar("sync")
        for dc in range(NCC):
            b.vbar("sync")
            v_xcl = b.dma("sync", xcg, xc_dst[dc * 128:(dc + 1) * 128, :])
            for hf in range(2):
                b.sbar("ten")

                def mm_dt(e, dc=dc, hf=hf):
                    last = None
                    for lq in range(2):
                        for rc in range(2):
                            last = e.matmul(
                                pB[:, lq * 512:(lq + 1) * 512],
                                dtw_v[:, (rc * NCC + dc) * 128:(rc * NCC + dc + 1) * 128],
                                dtr_t[:, rc * 2048 + hf * 1024 + lq * 512:
                                      rc * 2048 + hf * 1024 + (lq + 1) * 512],
                                start=(rc == 0), stop=(rc == 1))
                    return last
                v_dm = b.op("ten", mm_dt, inc=True)
                b.wait("sca", b.s_mm, v_dm)
                b.op("sca", lambda e, dc=dc, hf=hf: e.activation(
                    dte_t[:, hf * 1024:(hf + 1) * 1024], pB[:], AF.Exp,
                    bias=dtbias_s[:, dc:dc + 1]))
            b.stbar("sca")
            v_sp = b.op("sca", lambda e: e.activation(dt_t, dte_t, AF.Ln, bias=onesf_s[:]), inc=True)
            b.wait("vec", b.s_sca, v_sp)
            b.waitt("vec", v_xcl)
            v_dtx = b.op("vec", lambda e: e.tensor_tensor(dtx_t, dt_t, xcg, ALU.mult),
                         inc=True)
            b.wait("sync", b.s_vec, v_dtx)
            b.store(dt_dst[dc * 128:(dc + 1) * 128, :], dt_t)
            b.store(dtx_dst[dc * 128:(dc + 1) * 128, :], dtx_t)
        b.fence()

    if phases >= 3:
        b.scope("Bdup")
        phB(ln1p_d, True)
    if phases >= 4:
        b.scope("B")
        phB(ln1_d, False)

    def suffix_state():
        onesw = BUF2[:, 0:1024].bitcast(BF16)
        cumt = [BUF2[:, 1024 + i * 2048: 1024 + (i + 1) * 2048] for i in range(2)]
        dtt = [BUF2[:, 5120 + i * 1024: 5120 + (i + 1) * 1024].bitcast(BF16)
               for i in range(2)]
        wsuf = [BUF2[:, 7168 + i * 1024: 7168 + (i + 1) * 1024].bitcast(BF16)
                for i in range(2)]
        dbxp = [BUF2[:, 9216 + i * 1024: 9216 + (i + 1) * 1024].bitcast(BF16)
                for i in range(2)]
        scrx = BUF2[:, 11264:12288].bitcast(BF16)
        dtxt = [BUF2[:, 12288 + i * 1024: 12288 + (i + 1) * 1024].bitcast(BF16)
                for i in range(2)]
        negA_s = BUF3[:, 0:1024]
        biasA_s = BUF3[:, 1024:2048]
        b.op("vec", lambda e: e.memset(onesw, 1.0))
        b.op("vec", lambda e: e.tensor_scalar_mul(negA_s, A_s[:], -1.0))
        pv = {}
        for tp in range(NCC):
            pr = tp % 2
            if ("cum", pr) in pv:
                b.wait("sync", b.s_vec, pv[("cum", pr)])
            v_l = b.dma("sync", dtt[pr], dtp_d[tp * 128:(tp + 1) * 128, :])
            b.waitt("vec", v_l)
            if ("cst", pr) in pv:
                b.waitt("vec", pv[("cst", pr)])
            v_sc = b.op("vec", lambda e, pr=pr: e.tensor_tensor_scan(
                cumt[pr], onesw, dtt[pr], 0.0, ALU.mult, ALU.add), inc=True)
            v_c = b.op("vec", lambda e, tp=tp, pr=pr: e.tensor_copy(
                cumend_s[:, tp:tp + 1], cumt[pr][:, 2047:2048]), inc=True)
            pv[("cum", pr)] = v_c
            b.wait("sync", b.s_vec, v_c)
            pv[("cst", pr)] = b.store(cum_d[tp * 128:(tp + 1) * 128, :], cumt[pr])
        for n in range(NST):
            b.op("vec", lambda e, n=n: e.tensor_tensor(
                biasA_s.rearrange("p (a s) -> p a s", s=NST)[:, :, n],
                A_s[:].rearrange("p (a s) -> p a s", s=NST)[:, :, n],
                cumend_s[:], ALU.mult))
        v_ba = b.op("vec", lambda e: e.tensor_copy(tcol[0][:], tcol[0][:]), inc=True)
        b.wait("sca", b.s_vec, v_ba)
        brep = BUF1[:].bitcast(BF16)
        v_br = b.dma("gp", brep.rearrange("p (s l) -> p s l", s=NST),
                     Bnp_d.unsqueeze(0).broadcast_to([128, 16, TOK]))
        b.waitt("vec", v_br)
        b.waitt("gp", v_br)
        b.stbar("sync")
        pv = {}
        for tp in range(NCC):
            pr = tp % 2
            if ("st", pr) in pv:
                b.wait("sync", b.s_vec, pv[("st", pr)])
                b.wait("sync", b.s_sca, pv[("w", pr)])
            v_c = b.dma("sync", cumt[pr], cum_d[tp * 128:(tp + 1) * 128, :])
            v_x = b.dma("sync", dtxt[pr], dtxp_d[tp * 128:(tp + 1) * 128, :])
            b.waitt("sca", v_c)
            b.waitt("vec", v_x)
            for n in range(NST):
                col = tp * NST + n
                npr = n % 2
                if ("stt", npr) in pv:
                    b.wait("sca", b.s_vec, pv[("stt", npr)])
                v_w = b.op("sca", lambda e, col=col, npr=npr, pr=pr: e.activation(
                    wsuf[npr], cumt[pr], AF.Exp, scale=negA_s[:, col:col + 1],
                    bias=biasA_s[:, col:col + 1]), inc=True)
                pv[("w", pr)] = v_w
                v_db = b.op("vec", lambda e, n=n, npr=npr, pr=pr: e.tensor_tensor(
                    dbxp[npr], dtxt[pr], brep[:, n * TOK:(n + 1) * TOK], ALU.mult),
                    inc=True)
                b.wait("vec", b.s_sca, v_w)
                v_st = b.op("vec", lambda e, col=col, npr=npr: e.scalar_tensor_tensor(
                    scrx, wsuf[npr], 1.0, dbxp[npr], ALU.mult, ALU.mult,
                    accum_out=state_s[:, col:col + 1]), inc=True)
                pv[("stt", npr)] = v_st
                pv[("st", pr)] = v_st
        b.op("vec", lambda e: e.tensor_scalar_mul(state_s[:], state_s[:], par_s[:]),
             inc=True)
        b.fence()

    if phases >= 5:
        b.scope("suffix")
        suffix_state()

    def scan_phase():
        brep = BUF1[:, 0:8192].bitcast(BF16)
        crep = BUF1[:, 8192:16384].bitcast(BF16)
        dbx = [BUF2[:, 0:4096].bitcast(BF16),
               BUF2[:, 4096:8192].bitcast(BF16)]
        yt = BUF3[:].bitcast(BF16)
        dA = [BUF2[:, 8192 + i * 1024: 8192 + (i + 1) * 1024] for i in range(4)]
        dtt = [BUF2[:, 12288 + i * 512: 12288 + (i + 1) * 512].bitcast(BF16)
               for i in range(2)]
        dtxt = [BUF2[:, 13312 + i * 512: 13312 + (i + 1) * 512].bitcast(BF16)
                for i in range(2)]
        xcg = [BUF2[:, 14336 + i * 512: 14336 + (i + 1) * 512].bitcast(BF16)
               for i in range(2)]
        ztt = BUF2[:, 15360:15872].bitcast(BF16)
        szt = BUF2[:, 15872:16384].bitcast(BF16)
        yf = yt[:, 0:1024]
        ygo = yt[:, 1024:2048]
        prev = {}
        for hf in range(2):
            sl = slice(hf * 1024, (hf + 1) * 1024)
            b.vbar("gp")
            v_br = b.dma("gp", brep.rearrange("p (s l) -> p s l", s=NST),
                         Bn_d[:, sl].unsqueeze(0).broadcast_to([128, 16, 1024]))
            v_cr = b.dma("gp", crep.rearrange("p (s l) -> p s l", s=NST),
                         Cn_d[:, sl].unsqueeze(0).broadcast_to([128, 16, 1024]))
            for tp in range(NCC):
                pr = tp % 2
                rows = slice(tp * 128, (tp + 1) * 128)
                if ("sz", pr) in prev:
                    b.wait("sync", b.s_sca, prev[("sz", pr)])
                    b.wait("sync", b.s_vec, prev[("yg", pr)])
                    b.wait("sync", b.s_vec, prev[("db", pr)])
                v_dt = b.dma("sync", dtt[pr], dt_d[rows, sl])
                v_dx = b.dma("sync", dtxt[pr], dtx_d[rows, sl])
                v_xc = b.dma("sync", xcg[pr], xc_d[rows, sl])
                if ("sz", 1 - pr) in prev:
                    b.wait("sync", b.s_sca, prev[("sz", 1 - pr)])
                v_zt = b.dma("sync", ztt, z_d[rows, sl])
                b.waitt("vec", v_dx)
                if tp == 0:
                    b.waitt("vec", v_br)
                    b.waitt("vec", v_cr)
                for dh in range(2):
                    v_db = b.op("vec", lambda e, pr=pr, dh=dh: e.tensor_tensor(
                        dbx[dh].rearrange("p (s l) -> p s l", s=8),
                        dtxt[pr].unsqueeze(1).broadcast_to([128, 8, 1024]),
                        brep.rearrange("p (s l) -> p s l", s=NST)[:, dh * 8:(dh + 1) * 8, :],
                        ALU.mult), inc=True)
                prev[("db", pr)] = v_db
                b.waitt("sca", v_dt)
                for n in range(NST):
                    col = tp * NST + n
                    dh = n // 8
                    if ("scan", col % 4) in prev:
                        b.wait("sca", b.s_vec, prev[("scan", col % 4)])
                    v_a = b.op("sca", lambda e, pr=pr, col=col: e.activation(
                        dA[col % 4], dtt[pr], AF.Exp, scale=A_s[:, col:col + 1]),
                        inc=True)
                    if n == 1 and ("ygst", 0) in prev:
                        b.waitt("vec", prev[("ygst", 0)])
                    b.wait("vec", b.s_sca, v_a)
                    v_s = b.op("vec", lambda e, n=n, col=col, dh=dh: e.tensor_tensor_scan(
                        yt[:, n * 1024:(n + 1) * 1024], dA[col % 4],
                        dbx[dh][:, (n % 8) * 1024:(n % 8 + 1) * 1024],
                        state_s[:, col:col + 1], ALU.mult, ALU.add), inc=True)
                    prev[("scan", col % 4)] = v_s
                    if n % 8 == 7:
                        prev[("scanh", dh)] = v_s
                prev[("scans", 0)] = v_s
                if hf == 0:
                    b.op("vec", lambda e, tp=tp: e.tensor_copy(
                        state_s[:].rearrange("p (a s) -> p a s", s=NST)[:, tp, :],
                        yt[:].rearrange("p (s l) -> p s l", s=NST)[:, :, 1023]))
                b.op("vec", lambda e: e.tensor_tensor(yt[:], yt[:], crep[:], ALU.mult))
                b.op("vec", lambda e: e.tensor_tensor(
                    yt[:, 0:8192], yt[:, 0:8192], yt[:, 8192:16384], ALU.add))
                b.op("vec", lambda e: e.tensor_tensor(
                    yt[:, 0:4096], yt[:, 0:4096], yt[:, 4096:8192], ALU.add))
                b.op("vec", lambda e: e.tensor_tensor(
                    yt[:, 0:2048], yt[:, 0:2048], yt[:, 2048:4096], ALU.add))
                v_tr = b.op("vec", lambda e: e.tensor_tensor(
                    yf, yt[:, 0:1024], yt[:, 1024:2048], ALU.add), inc=True)
                b.waitt("sca", v_zt)
                if ("yg", 1 - pr) in prev:
                    b.wait("sca", b.s_vec, prev[("yg", 1 - pr)])
                v_sz = b.op("sca", lambda e: e.activation(szt, ztt, AF.Silu),
                            inc=True)
                prev[("sz", pr)] = v_sz
                b.waitt("vec", v_xc)
                b.wait("vec", b.s_sca, v_sz)
                b.op("vec", lambda e, pr=pr, tp=tp: e.scalar_tensor_tensor(
                    yf, xcg[pr], Dpar_s[:, tp:tp + 1], yf, ALU.mult, ALU.add))
                v_yg = b.op("vec", lambda e: e.tensor_tensor(ygo, yf, szt,
                                                             ALU.mult), inc=True)
                prev[("yg", pr)] = v_yg
                b.wait("sync", b.s_vec, v_yg)
                prev[("ygst", 0)] = b.store(yg_d[rows, sl], ygo)
        b.fence()

    if phases >= 6:
        b.scope("scan")
        scan_phase()

    def outproj_phase():
        ygslab = BUF1[:].bitcast(BF16)
        woA = [BUF3[:, 0:4096].bitcast(BF16), BUF3[:, 4096:8192].bitcast(BF16)]
        woB = [BUF2[:, 2048:6144].bitcast(BF16), BUF2[:, 6144:10240].bitcast(BF16)]
        x1t = BUF2[:, 0:2048]
        v_st2 = None
        pvo = {}
        for lb in range(4):
            sl = slice(lb * 512, (lb + 1) * 512)
            b.stbar("sync")
            v_slab = b.dma("sync", ygslab[:, 0:NCC * 512]
                           .rearrange("p (a l) -> p a l", a=NCC),
                           yg_d[:, sl].rearrange("(a p) l -> p a l", p=128))
            b.waitt("ten", v_slab)
            for hp in range(NHC // 2):
                wo = woA if hp % 2 == 0 else woB
                wtoks = []
                for h2 in range(2):
                    hc = hp * 2 + h2
                    if pvo.get("mm2"):
                        b.wait("gp", b.s_mm, pvo["mm2"])
                    wtoks.append(b.dma("gp", wo[h2], WoutT[:, hc, :]))
                v_w0, v_w1 = wtoks
                b.waitt("ten", v_w0)
                b.waitt("ten", v_w1)
                b.vbar("ten")

                def mm_o(e, hp=hp, wo=wo):
                    last = None
                    for cg in range(NCC):
                        for h2 in range(2):
                            last = e.matmul([pA0, pA1][h2][:, 0:512],
                                            wo[h2].rearrange("p (a c) -> p a c", a=NCC)
                                            [:, cg, :],
                                            ygslab[:, cg * 512:(cg + 1) * 512],
                                            start=(cg == 0), stop=(cg == NCC - 1))
                    return last
                v_m = b.op("ten", mm_o, inc=True)
                pvo["mm2"] = pvo.get("mm1")
                pvo["mm1"] = v_m
                b.vbar("sync")
                b.waitt("sync", v_st2)
                v_x1 = b.dma("sync", x1t[:, 0:1024].rearrange("p (a l) -> p a l", a=2),
                             x1_d[:, sl].rearrange("(a p) l -> p a l", p=128)
                             [:, hp * 2:hp * 2 + 2, :])
                b.wait("vec", b.s_mm, v_m)
                b.waitt("vec", v_x1)
                v_x2 = None
                for h2 in range(2):
                    v_x2 = b.op("vec", lambda e, h2=h2: e.scalar_tensor_tensor(
                        x1t[:, h2 * 512:(h2 + 1) * 512], [pA0, pA1][h2][:, 0:512], 1.0,
                        x1t[:, h2 * 512:(h2 + 1) * 512], ALU.mult, ALU.add), inc=True)
                b.wait("sync", b.s_vec, v_x2)
                v_st2 = b.store(x2_d[:, sl].rearrange("(a p) l -> p a l", p=128)
                                [:, hp * 2:hp * 2 + 2, :],
                                x1t[:, 0:1024].rearrange("p (a l) -> p a l", a=2))
        b.fence()

    if phases >= 7:
        b.scope("outproj")
        outproj_phase()

    vla2 = b.dma("gp", BUF3[:].bitcast(BF16), lAT)
    b.waitt("ten", vla2)
    if phases >= 8:
        b.scope("D")
        phaseA(x2_d, out, None, BUF3[:].bitcast(BF16))

    if probe is not None:
        b.fence()
        probes = dict(x1=x1_d, ln1=ln1_d, ln1p=ln1p_d, xc=xc_d, z=z_d, dt=dt_d,
                      dtx=dtx_d, dtp=dtp_d, dtxp=dtxp_d, xcp=xcp_d, cum=cum_d,
                      yg=yg_d, x2=x2_d, xh=xh_d, w8=w8_d, rows=rows_d)
        pa = probes[probe]
        rows = min(pa.shape[0], H)
        cols = min(pa.shape[1], TOK)
        b.dma("gp", out[0:rows, 0:cols], pa[0:rows, 0:cols])
        if probe in ("cum",):
            b.dma("gp", out[rows: 2 * rows if 2 * rows <= H else H, 0:cols],
                  pa[rows:2 * rows, 0:cols]) if pa.shape[0] > rows and 2 * rows <= H else None
        b.wait("gp", b.s_gld, b.val(b.s_gld))

    b.wait("sync", b.s_st, b.val(b.s_st))

    def run_stream(e, lst):
        opened = None
        for fn in lst:
            if isinstance(fn, tuple) and fn[0] == "SCOPE":
                if opened:
                    nc.pop_named_scope(opened)
                nc.push_named_scope(fn[1])
                opened = fn[1]
            else:
                fn(e)
        if opened:
            nc.pop_named_scope(opened)

    with nc.Block() as block:
        @block.sync
        def _(e):
            run_stream(e, b.E["sync"])

        @block.gpsimd
        def _(e):
            run_stream(e, b.E["gp"])

        @block.vector
        def _(e):
            run_stream(e, b.E["vec"])

        @block.scalar
        def _(e):
            run_stream(e, b.E["sca"])

        @block.tensor
        def _(e):
            run_stream(e, b.E["ten"])

    b.es.close()
    return nc


_NC_CACHE = None


def kernel(L0_attn_hidden_states, L0_attn_in_proj_w, L0_attn_conv_w,
           L0_attn_conv_b, L0_attn_x_proj_w, L0_attn_dt_proj_w,
           L0_attn_dt_bias, L0_attn_A_log, L0_attn_D_param,
           L0_attn_out_proj_w, L0_attn_dt_ln_w, L0_attn_B_ln_w,
           L0_attn_C_ln_w, L0_ffn_gate_w, L0_ffn_lora_A, L0_ffn_lora_B):
    global _NC_CACHE
    x = np.asarray(L0_attn_hidden_states, dtype=np.float32)
    Win0 = np.asarray(L0_attn_in_proj_w, np.float32)
    Win = np.ascontiguousarray(
        Win0.reshape(NHC, 128, 2 * NCC, 128).transpose(1, 2, 0, 3)
        .reshape(128, 2 * NCC, NHC * 128))
    Wout0 = np.ascontiguousarray(np.asarray(L0_attn_out_proj_w, np.float32).T)
    WoutT = np.ascontiguousarray(
        Wout0.reshape(NCC, 128, NHC, 128).transpose(1, 2, 0, 3)
        .reshape(128, NHC, NCC * 128))
    xpwT = np.ascontiguousarray(np.asarray(L0_attn_x_proj_w, np.float32).T)
    dtw0 = np.ascontiguousarray(np.asarray(L0_attn_dt_proj_w, np.float32).T)
    dtwT = np.ascontiguousarray(
        dtw0.reshape(2, 128, NCC, 128).transpose(1, 0, 2, 3).reshape(128, 2 * NCC * 128))
    lA0 = np.transpose(np.asarray(L0_ffn_lora_A, np.float32), (0, 2, 1))
    lAT = np.ascontiguousarray(
        lA0.reshape(NE, NHC, 128, ER).transpose(2, 0, 1, 3).reshape(128, NE * NHC * ER))
    lBT = np.ascontiguousarray(np.transpose(np.asarray(L0_ffn_lora_B, np.float32), (0, 2, 1)))
    gateT = np.ascontiguousarray(np.asarray(L0_ffn_gate_w, np.float32).T)
    conv_w = np.asarray(L0_attn_conv_w, np.float32)
    conv_b = np.asarray(L0_attn_conv_b, np.float32)
    dt_bias = np.asarray(L0_attn_dt_bias, np.float32)
    A_log = np.asarray(L0_attn_A_log, np.float32)
    D_param = np.asarray(L0_attn_D_param, np.float32)
    dt_ln_w = np.asarray(L0_attn_dt_ln_w, np.float32)
    B_ln_w = np.asarray(L0_attn_B_ln_w, np.float32)
    C_ln_w = np.asarray(L0_attn_C_ln_w, np.float32)

    convw = np.ascontiguousarray(
        conv_w.reshape(NCC, 128, 4).transpose(1, 0, 2).reshape(128, NCC * 4))
    convb = np.ascontiguousarray(conv_b.reshape(NCC, 128).T)
    dtbias = np.ascontiguousarray(dt_bias.reshape(NCC, 128).T)
    Dparh = np.ascontiguousarray(D_param.reshape(NCC, 128).T)
    AlogT = np.ascontiguousarray(
        A_log.reshape(NCC, 128, NST).transpose(1, 0, 2).reshape(128, NCC * NST))
    dtlnw = np.ascontiguousarray(dt_ln_w.reshape(2, 128).T)
    Blnw = np.ascontiguousarray(B_ln_w.reshape(16, 1))
    Clnw = np.ascontiguousarray(C_ln_w.reshape(16, 1))
    ident8 = np.eye(8, dtype=np.float32)
    idw128 = np.eye(128, dtype=np.float32)
    ones1 = np.ones((128, 1), np.float32)

    if _NC_CACHE is None:
        _NC_CACHE = build()
    nc = _NC_CACHE

    in_maps = []
    for c in range(8):
        bb, half = c // 2, c % 2
        xTb = np.ascontiguousarray(x[bb].T)
        in_maps.append(dict(
            xT=np.ascontiguousarray(xTb[:, half * TOK:(half + 1) * TOK]),
            xTp=np.ascontiguousarray(xTb[:, 0:TOK]),
            parity=np.full((128, 1), float(half), np.float32),
            Win=Win, WoutT=WoutT, xpwT=xpwT, dtwT=dtwT, lAT=lAT, lBT=lBT,
            gateT=gateT, convw=convw, convb=convb, dtbias=dtbias, Dpar=Dparh,
            AlogT=AlogT, dtlnw=dtlnw, Blnw=Blnw, Clnw=Clnw,
            ident8=ident8, idw128=idw128, ones1=ones1,
        ))

    res = run_bass_kernel_spmd(nc, in_maps, core_ids=list(range(8)))
    outarr = np.zeros((4, 4096, 4096), np.float32)
    for c in range(8):
        bb, half = c // 2, c % 2
        outarr[bb, half * TOK:(half + 1) * TOK, :] = res.results[c]["out"].T
    return outarr
